# version 01
# baseline (speedup 1.0000x reference)
import numpy as np
import ml_dtypes
import concourse.bass as bass
import concourse.mybir as mybir
from concourse.bass_utils import run_bass_kernel_spmd
from contextlib import ExitStack

F32 = mybir.dt.float32
BF16 = mybir.dt.bfloat16
AF = mybir.ActivationFunctionType
ALU = mybir.AluOpType
AX = mybir.AxisListType

NCORES = 8
B = 2
N = 2048
D = 2048
H = 16
DH = 128
DFF = 8192
DC = D // 128
FCN = DFF // 128
EPS = 1e-5
TANH_SCALE = (DH ** -0.5) / 50.0
NEG = -1.0e9

ENGS = ("sync", "tensor", "vector", "scalar", "gpsimd")


_PHASE_N = [0]

def _op_real(tok):
    import os
    v = os.environ.get("K_OPS")
    if v is None:
        return True
    return tok in v.split(",")



class Phase:
    def __init__(self):
        self.ops = {e: [] for e in ENGS}

    def add(self, eng, fn):
        self.ops[eng].append(fn)

    def flush(self, block):
        import os
        _PHASE_N[0] += 1
        limit = int(os.environ.get("K_UPTO", "1000000"))
        if _PHASE_N[0] > limit:
            self.ops = {e: [] for e in ENGS}
            return
        for e in ENGS:
            fns = self.ops[e]
            if fns:
                getattr(block, e)(lambda en, fns=fns: [f(en) for f in fns])
        self.ops = {e: [] for e in ENGS}


def build(dump=False):
    nc = bass.Bass("TRN2", target_bir_lowering=False)

    x_in = nc.dram_tensor("x", [B, N, D], F32, kind="ExternalInput").ap()
    wqkv = nc.dram_tensor("wqkv", [D, 3 * D], F32, kind="ExternalInput").ap()
    wg_in = nc.dram_tensor("wg", [D, H], F32, kind="ExternalInput").ap()
    wout_in = nc.dram_tensor("wout", [D, D], F32, kind="ExternalInput").ap()
    w1_in = nc.dram_tensor("w1", [DFF, D], F32, kind="ExternalInput").ap()
    b1_in = nc.dram_tensor("b1", [DFF], F32, kind="ExternalInput").ap()
    w2_in = nc.dram_tensor("w2", [D, DFF], F32, kind="ExternalInput").ap()
    b2_in = nc.dram_tensor("b2", [D], F32, kind="ExternalInput").ap()
    out = nc.dram_tensor("out", [B, N, D], F32, kind="ExternalOutput").ap()

    sk = "ExternalOutput" if dump else "Internal"
    xnT_s = nc.dram_tensor("xnT_s", [B, DC, 128, N], BF16, kind=sk).ap()
    qkT_s = nc.dram_tensor("qkT_s", [B, H, 2, 128, N], BF16, kind=sk).ap()
    v_s = nc.dram_tensor("v_s", [B, N, D], BF16, kind=sk).ap()
    gates_s = nc.dram_tensor("gates_s", [H, B * N], BF16, kind=sk).ap()
    s_s = nc.dram_tensor("s_s", [B, N, D], BF16, kind=sk).ap()
    w1qT_s = nc.dram_tensor("w1qT_s", [DC, 128, DFF], BF16, kind=sk).ap()
    w2qT_s = nc.dram_tensor("w2qT_s", [DFF, D], BF16, kind=sk).ap()

    identf_c = nc.inline_tensor(np.eye(128, dtype=np.float32), "identf").ap()
    identb_c = nc.inline_tensor(np.eye(128, dtype=ml_dtypes.bfloat16), "identb").ap()
    ones_row_c = nc.inline_tensor(np.ones((1, 128), dtype=ml_dtypes.bfloat16), "onesrow").ap()
    ones_col_c = nc.inline_tensor(np.ones((128, 1), dtype=ml_dtypes.bfloat16), "onescol").ap()
    ones_row_f_c = nc.inline_tensor(np.ones((1, 128), dtype=np.float32), "onesrowf").ap()
    kkn = np.arange(128)[:, None]
    qqn = np.arange(512)[None, :]
    masks_np = np.stack([np.where(kkn + r * 128 > qqn, NEG, 0.0).astype(np.float32)
                         for r in range(4)])
    masks_c = nc.inline_tensor(masks_np, "masks").ap()
    taubias_c = nc.inline_tensor(np.array([[-1e-8, 0.5e-8, -0.5e-8]], dtype=np.float32),
                                 "taubias").ap()

    sems = {}
    cnt = {}

    def inc(name, k=1):
        cnt[name] += k
        return cnt[name]

    with ExitStack() as stack:
        ee = stack.enter_context

        for name in ("bar", "c_ld",
                     "x_ld", "ln_st", "ln_bn", "ln_norm", "ln_tp", "ln_ev", "xnT_st",
                     "w_ld", "qk_mm", "qk_ev", "qk_st", "g_mm",
                     "v_mm", "v_ev", "v_st",
                     "at_ld", "at_sim", "at_tanh", "at_mask", "at_exp",
                     "at_av", "at_scl", "at_bc", "at_bcev", "at_ot",
                     "s4_mm", "s4_ev", "s4_st",
                     "t5_ld", "t5_abs", "t5_tau", "t5_tern", "t5_tp",
                     "t5_ev", "t5_st",
                     "f6_xnb", "f6_w1", "f6_w2", "f6_h", "f6_gelu",
                     "f6_y", "f6_yev", "f6_silu", "f6_sld", "f6_xld",
                     "f6_cmb", "f6_out"):
            sems[name] = ee(nc.semaphore(name))
            cnt[name] = 0

        def barrier(ph):
            tgt = cnt["bar"] + 5
            cnt["bar"] = tgt
            hh = sems["bar"]
            for e in ENGS:
                ph.add(e, lambda en, t=tgt: (en.sem_inc(hh, 1), en.wait_ge(hh, t)))

        identf = ee(nc.sbuf_tensor("identf_b", [128, 128], F32))
        identb = ee(nc.sbuf_tensor("identb_b", [128, 128], BF16))
        ones_row = ee(nc.sbuf_tensor("ones_row_b", [1, 128], BF16))
        ones_col = ee(nc.sbuf_tensor("ones_col_b", [128, 1], BF16))
        ones_row_f = ee(nc.sbuf_tensor("ones_row_f_b", [1, 128], F32))
        masks = ee(nc.sbuf_tensor("masks_b", [128, 4 * 512], F32))
        b1_sb = ee(nc.sbuf_tensor("b1_sb", [128, FCN], F32))
        b2row = ee(nc.sbuf_tensor("b2row_b", [1, D], F32))
        taub = ee(nc.sbuf_tensor("taub_b", [1, 3], F32))

        block = ee(nc.Block())

        ph = Phase()

        def _setup(sy):
            for dst, src in ((identf[:], identf_c), (identb[:], identb_c),
                             (ones_row[:], ones_row_c), (ones_col[:], ones_col_c),
                             (ones_row_f[:], ones_row_f_c),
                             (b2row[:], b2_in.rearrange("(one d) -> one d", one=1)),
                             (taub[:], taubias_c)):
                sy.dma_start(out=dst, in_=src).then_inc(sems["c_ld"], 16)
            sy.dma_start(out=masks[:].rearrange("p (r q) -> p r q", r=4),
                         in_=masks_c.rearrange("r p q -> p r q")).then_inc(sems["c_ld"], 16)
        ph.add("sync", _setup)
        inc("c_ld", 8 * 16)

        def _setup_gp(gp):
            gp.dma_start(out=b1_sb[:], in_=b1_in.rearrange("(fc p) -> p fc", p=128),
                         allow_slow_non_contiguous=True).then_inc(sems["c_ld"], 16)
        ph.add("gpsimd", _setup_gp)
        inc("c_ld", 16)
        barrier(ph)
        ph.flush(block)

        with ExitStack() as scps:
            psA = scps.enter_context(nc.psum_tensor("psA", [128, 2048], F32))
            psB = scps.enter_context(nc.psum_tensor("psB", [128, 2048], F32))
            for b in range(B):
                with ExitStack() as sca:
                    ea = sca.enter_context
                    xnT_sb = ea(nc.sbuf_tensor(f"xnT_sb{b}", [128, DC * N], BF16))
                    wbig = ea(nc.sbuf_tensor(f"wv_sb{b}", [128, DC * D], BF16))
                    wq4 = [ea(nc.sbuf_tensor(f"wq4_{b}_{i}", [128, DC * 256], BF16))
                           for i in range(2)]
                    xf = [ea(nc.sbuf_tensor(f"xf{b}_{i}", [128, D], F32)) for i in range(2)]
                    xn_t = [ea(nc.sbuf_tensor(f"xnt{b}_0", [128, D], F32))]
                    bnst = ea(nc.sbuf_tensor(f"bnst{b}", [128, 24], F32))
                    bnmv = ea(nc.sbuf_tensor(f"bnmv{b}", [128, 2], F32))
                    veps = ea(nc.sbuf_tensor(f"veps{b}", [128, 1], F32))
                    sdv = ea(nc.sbuf_tensor(f"sdv{b}", [128, 1], F32))
                    rinv = ea(nc.sbuf_tensor(f"rinv{b}", [128, 1], F32))
                    stage = [ea(nc.sbuf_tensor(f"stageA{b}_{i}", [128, D], BF16))
                             for i in range(2)]
                    gstage = ea(nc.sbuf_tensor(f"gstage{b}", [16, N], BF16))
                    wgsb = ea(nc.sbuf_tensor(f"wgsb{b}", [128, DC * 16], BF16))

                    xnT_view = xnT_sb[:].rearrange("p (dc t) -> p dc t", dc=DC)

                    ph = Phase()
                    b_norm = cnt["ln_norm"]
                    b_tp = cnt["ln_tp"]
                    b_ev = cnt["ln_ev"]
                    b_st = cnt["ln_st"]
                    b_xld = cnt["x_ld"]
                    for tt in range(16):
                        sl = tt % 2

                        def _ld(sy, b=b, tt=tt, sl=sl, nw=b_norm + tt - 1):
                            if tt >= 2:
                                sy.wait_ge(sems["ln_norm"], nw)
                            sy.dma_start(out=xf[sl][:],
                                         in_=x_in[b, tt * 128:(tt + 1) * 128, :]
                                         ).then_inc(sems["x_ld"], 16)
                        ph.add("sync", _ld)
                        x_ld_v = inc("x_ld", 16)

                        bnb = cnt["ln_bn"]

                        def _ln(ve, sl=sl, wv=x_ld_v, bnb=bnb):
                            ve.wait_ge(sems["x_ld"], wv)
                            k = bnb
                            for c in range(4):
                                ve.bn_stats(bnst[:, c * 6:(c + 1) * 6],
                                            xf[sl][:, c * 512:(c + 1) * 512]
                                            ).then_inc(sems["ln_bn"], 1)
                                k += 1
                                ve.wait_ge(sems["ln_bn"], k)
                            ve.bn_aggr(bnmv[:], bnst[:]).then_inc(sems["ln_bn"], 1)
                            k += 1
                            ve.wait_ge(sems["ln_bn"], k)
                            ve.tensor_scalar_add(veps[:], bnmv[:, 1:2], EPS
                                                 ).then_inc(sems["ln_st"], 1)
                        ph.add("vector", _ln)
                        inc("ln_bn", 5)
                        v1 = inc("ln_st", 1)

                        def _sq(sc, wv=v1):
                            sc.wait_ge(sems["ln_st"], wv)
                            sc.sqrt(sdv[:], veps[:]).then_inc(sems["ln_st"], 1)
                        ph.add("scalar", _sq)
                        v2 = inc("ln_st", 1)

                        def _nrm(ve, sl=sl, wv=v2, tpw=b_tp + 4 * tt, tt=tt):
                            ve.wait_ge(sems["ln_st"], wv)
                            ve.reciprocal(rinv[:], sdv[:])
                            if tt >= 1:
                                ve.wait_ge(sems["ln_tp"], tpw)
                            ve.tensor_scalar(xn_t[0][:], xf[sl][:], bnmv[:, 0:1],
                                             rinv[:], ALU.subtract, ALU.mult
                                             ).then_inc(sems["ln_norm"], 1)
                        ph.add("vector", _nrm)
                        norm_v = inc("ln_norm", 1)

                        for g in range(4):
                            gi = tt * 4 + g
                            pbank = (psA if gi % 2 == 0 else psB)

                            def _tp(te, g=g, gi=gi, pbank=pbank, wv=norm_v,
                                    evw=b_ev + gi - 1):
                                te.wait_ge(sems["ln_norm"], wv)
                                if gi >= 2:
                                    te.wait_ge(sems["ln_ev"], evw)
                                last = None
                                for j in range(4):
                                    dc = g * 4 + j
                                    last = te.transpose(pbank[:, j * 128:(j + 1) * 128],
                                                        xn_t[0][:, dc * 128:(dc + 1) * 128],
                                                        identf[:])
                                last.then_inc(sems["ln_tp"], 1)
                            ph.add("tensor", _tp)
                            tp_v = inc("ln_tp", 1)

                            def _ev(sc, g=g, tt=tt, pbank=pbank, wv=tp_v):
                                sc.wait_ge(sems["ln_tp"], wv)
                                dst = xnT_view[:, g * 4:(g + 1) * 4, tt * 128:(tt + 1) * 128]
                                src = pbank[:, 0:512].rearrange("p (j t) -> p j t", j=4)
                                sc.copy(dst, src).then_inc(sems["ln_ev"], 1)
                            ph.add("scalar", _ev)
                            inc("ln_ev", 1)

                    def _xst(sy, b=b, wv=cnt["ln_ev"]):
                        sy.wait_ge(sems["ln_ev"], wv)
                        sy.dma_start(out=xnT_s[b].rearrange("dc p t -> p dc t"),
                                     in_=xnT_view).then_inc(sems["xnT_st"], 16)
                    ph.add("sync", _xst)
                    inc("xnT_st", 16)
                    barrier(ph)
                    ph.flush(block)

                    ph = Phase()
                    b_qmm = cnt["qk_mm"]
                    b_qev = cnt["qk_ev"]
                    b_qst = cnt["qk_st"]
                    b_wld = cnt["w_ld"]
                    for qk in range(2):
                        for qu in range(8):
                            li = qk * 8 + qu

                            def _wld(gp, qk=qk, qu=qu, li=li,
                                     mmw=b_qmm + 2 * (li - 1)):
                                if li >= 2:
                                    gp.wait_ge(sems["qk_mm"], mmw)
                                gp.dma_start(
                                    out=wq4[li % 2][:].rearrange("p (dc m) -> p dc m", dc=DC),
                                    in_=wqkv[:, qk * D + qu * 256: qk * D + (qu + 1) * 256
                                             ].rearrange("(dc p) m -> p dc m", p=128)
                                ).then_inc(sems["w_ld"], 16)
                            ph.add("gpsimd", _wld)
                            w_ld_v = inc("w_ld", 16)

                            for hs in range(2):
                                h = qu * 2 + hs
                                g_idx = qk * 16 + h
                                pbank = psA if g_idx % 2 == 0 else psB

                                def _mm(te, li=li, hs=hs, pbank=pbank, wv=w_ld_v,
                                        evw=b_qev + g_idx - 1, g_idx=g_idx):
                                    te.wait_ge(sems["w_ld"], wv)
                                    if g_idx >= 2:
                                        te.wait_ge(sems["qk_ev"], evw)
                                    last = None
                                    for dc in range(DC):
                                        lhs = wq4[li % 2][:].rearrange(
                                            "p (dc m) -> p dc m", dc=DC)[
                                            :, dc, hs * 128:(hs + 1) * 128]
                                        for tq in range(4):
                                            last = te.matmul(
                                                pbank[:, tq * 512:(tq + 1) * 512], lhs,
                                                xnT_view[:, dc, tq * 512:(tq + 1) * 512],
                                                start=(dc == 0), stop=(dc == DC - 1))
                                    last.then_inc(sems["qk_mm"], 1)
                                ph.add("tensor", _mm)
                                mm_v = inc("qk_mm", 1)

                                def _ev(sc, g_idx=g_idx, pbank=pbank, wv=mm_v,
                                        stw=b_qst + 16 * (g_idx - 1)):
                                    sc.wait_ge(sems["qk_mm"], wv)
                                    if g_idx >= 2:
                                        sc.wait_ge(sems["qk_st"], stw)
                                    sc.copy(stage[g_idx % 2][:], pbank[:]
                                            ).then_inc(sems["qk_ev"], 1)
                                ph.add("scalar", _ev)
                                ev_v = inc("qk_ev", 1)

                                def _st(sy, b=b, h=h, qk=qk, g_idx=g_idx, wv=ev_v):
                                    sy.wait_ge(sems["qk_ev"], wv)
                                    sy.dma_start(out=qkT_s[b, h, qk],
                                                 in_=stage[g_idx % 2][:]
                                                 ).then_inc(sems["qk_st"], 16)
                                ph.add("sync", _st)
                                inc("qk_st", 16)

                    def _wgld(gp):
                        gp.dma_start(out=wgsb[:].rearrange("p (dc g) -> p dc g", g=16),
                                     in_=wg_in.rearrange("(dc p) g -> p dc g", p=128)
                                     ).then_inc(sems["w_ld"], 16)
                    ph.add("gpsimd", _wgld)
                    wg_v = inc("w_ld", 16)

                    def _gmm(te, wv=wg_v, evw=cnt["qk_ev"]):
                        te.wait_ge(sems["w_ld"], wv)
                        te.wait_ge(sems["qk_ev"], evw)
                        last = None
                        for tq in range(4):
                            for dc in range(DC):
                                last = te.matmul(
                                    psA[0:16, tq * 512:(tq + 1) * 512],
                                    wgsb[:].rearrange("p (dc g) -> p dc g", g=16)[:, dc, :],
                                    xnT_view[:, dc, tq * 512:(tq + 1) * 512],
                                    start=(dc == 0), stop=(dc == DC - 1))
                        last.then_inc(sems["g_mm"], 1)
                    ph.add("tensor", _gmm)
                    g1 = inc("g_mm", 1)

                    def _gsig(sc, wv=g1):
                        sc.wait_ge(sems["g_mm"], wv)
                        sc.activation(gstage[:], psA[0:16, :], AF.Sigmoid
                                      ).then_inc(sems["g_mm"], 1)
                    ph.add("scalar", _gsig)
                    g2 = inc("g_mm", 1)

                    def _gst(sy, b=b, wv=g2):
                        sy.wait_ge(sems["g_mm"], wv)
                        sy.dma_start(out=gates_s[:, b * N:(b + 1) * N],
                                     in_=gstage[:]).then_inc(sems["qk_st"], 16)
                    ph.add("sync", _gst)
                    inc("qk_st", 16)
                    barrier(ph)
                    ph.flush(block)

                    ph = Phase()
                    b_vmm = cnt["v_mm"]
                    b_vev = cnt["v_ev"]
                    b_vst = cnt["v_st"]

                    def _wvld(gp):
                        gp.dma_start(out=wbig[:].rearrange("p (dc m) -> p dc m", dc=DC),
                                     in_=wqkv[:, 2 * D:3 * D].rearrange(
                                         "(dc p) m -> p dc m", p=128)
                                     ).then_inc(sems["w_ld"], 16)
                    ph.add("gpsimd", _wvld)
                    wv_v = inc("w_ld", 16)

                    for tt in range(16):
                        pbank = psA if tt % 2 == 0 else psB

                        def _vmm(te, tt=tt, pbank=pbank, wv=wv_v, evw=b_vev + tt - 1):
                            te.wait_ge(sems["w_ld"], wv)
                            if tt >= 2:
                                te.wait_ge(sems["v_ev"], evw)
                            last = None
                            for dj in range(4):
                                for dc in range(DC):
                                    last = te.matmul(
                                        pbank[:, dj * 512:(dj + 1) * 512],
                                        xnT_view[:, dc, tt * 128:(tt + 1) * 128],
                                        wbig[:].rearrange("p (dc m) -> p dc m", dc=DC)[
                                            :, dc, dj * 512:(dj + 1) * 512],
                                        start=(dc == 0), stop=(dc == DC - 1))
                            last.then_inc(sems["v_mm"], 1)
                        ph.add("tensor", _vmm)
                        vm_v = inc("v_mm", 1)

                        def _vev(sc, tt=tt, pbank=pbank, wv=vm_v,
                                 stw=b_vst + 16 * (tt - 1)):
                            sc.wait_ge(sems["v_mm"], wv)
                            if tt >= 2:
                                sc.wait_ge(sems["v_st"], stw)
                            sc.copy(stage[tt % 2][:], pbank[:]).then_inc(sems["v_ev"], 1)
                        ph.add("scalar", _vev)
                        ve_v = inc("v_ev", 1)

                        def _vst(sy, b=b, tt=tt, wv=ve_v):
                            sy.wait_ge(sems["v_ev"], wv)
                            sy.dma_start(out=v_s[b, tt * 128:(tt + 1) * 128, :],
                                         in_=stage[tt % 2][:]).then_inc(sems["v_st"], 16)
                        ph.add("sync", _vst)
                        inc("v_st", 16)
                    barrier(ph)
                    ph.flush(block)

                with ExitStack() as scb:
                    eb = scb.enter_context
                    oaT = eb(nc.sbuf_tensor(f"oaT{b}", [128, H * N], BF16))
                    wout_sb = eb(nc.sbuf_tensor(f"wout_sb{b}", [128, H * D], BF16))
                    qT = [eb(nc.sbuf_tensor(f"qT{b}_{i}", [128, N], BF16)) for i in range(2)]
                    kTb = [eb(nc.sbuf_tensor(f"kT{b}_{i}", [128, N], BF16)) for i in range(2)]
                    vvb = [eb(nc.sbuf_tensor(f"vv{b}_{i}", [128, N], BF16)) for i in range(2)]
                    grow = [eb(nc.sbuf_tensor(f"grow{b}_{i}", [1, N], BF16)) for i in range(2)]
                    ttb = [eb(nc.sbuf_tensor(f"ttb{b}_{i}", [128, 512], F32)) for i in range(2)]
                    tmask = [eb(nc.sbuf_tensor(f"tmask{b}_{i}", [128, 512], F32))
                             for i in range(2)]
                    pTb = [eb(nc.sbuf_tensor(f"pTb{b}_{i}", [128, 512], BF16))
                           for i in range(4)]
                    rrow = eb(nc.sbuf_tensor(f"rrow{b}", [1, 512], F32))
                    srow = eb(nc.sbuf_tensor(f"srow{b}", [1, 512], BF16))
                    bcsb = [eb(nc.sbuf_tensor(f"bcsb{b}_{i}", [128, 512], BF16))
                            for i in range(2)]
                    stageB = [eb(nc.sbuf_tensor(f"stageB{b}_{i}", [128, D], BF16))
                              for i in range(2)]

                    psS = [psA[:, 0:512], psA[:, 512:1024]]
                    psO = psA[:, 1024:1536]
                    psR = psA[0:1, 1536:2048]
                    psBc = psB[:, 0:512]

                    ph = Phase()

                    def _woutld(gp, b=b):
                        gp.dma_start(out=wout_sb[:].rearrange("p (hc m) -> p hc m", hc=H),
                                     in_=wout_in.rearrange("(hc p) m -> p hc m", p=128)
                                     ).then_inc(sems["w_ld"], 16)
                    ph.add("gpsimd", _woutld)
                    wout_v = inc("w_ld", 16)

                    if b == 0:
                        masked_exp_after = []
                    for h in range(H):
                        sl = h % 2
                        gqb0 = cnt["at_ot"]

                        def _atld(sc, b=b, h=h, sl=sl, otw=cnt["at_ot"] - 4):
                            if h >= 2 or b > 0:
                                sc.wait_ge(sems["at_ot"], otw)
                            sc.dma_start(out=qT[sl][:], in_=qkT_s[b, h, 0]
                                         ).then_inc(sems["at_ld"], 16)
                            sc.dma_start(out=kTb[sl][:], in_=qkT_s[b, h, 1]
                                         ).then_inc(sems["at_ld"], 16)
                            sc.dma_start(out=vvb[sl][:].rearrange("p (kt j) -> p kt j", kt=16),
                                         in_=v_s[b, :, h * 128:(h + 1) * 128
                                                 ].rearrange("(kt p) j -> p kt j", p=128)
                                         ).then_inc(sems["at_ld"], 16)
                            sc.dma_start(out=grow[sl][:],
                                         in_=gates_s[h:h + 1, b * N:(b + 1) * N]
                                         ).then_inc(sems["at_ld"], 16)
                        ph.add("scalar", _atld)
                        at_ld_v = inc("at_ld", 64)

                        for qb in range(4):
                            nkt = 4 * (qb + 1)
                            gqb = cnt["at_ot"]
                            sim0 = cnt["at_sim"]
                            tanh0 = cnt["at_tanh"]
                            exp0 = cnt["at_exp"]
                            av0 = cnt["at_av"]
                            scl_prev = cnt["at_scl"]
                            bcev_prev = cnt["at_bcev"]

                            def _pe(te, sl=sl, qb=qb, nkt=nkt, ldv=at_ld_v,
                                    sim0=sim0, tanh0=tanh0, exp0=exp0, gqb=gqb,
                                    scl_prev=scl_prev):
                                te.wait_ge(sems["at_ld"], ldv)
                                if gqb >= 1:
                                    te.wait_ge(sems["at_ot"], gqb)
                                    te.wait_ge(sems["at_scl"], scl_prev)
                                for kt in range(nkt):
                                    if kt >= 2:
                                        te.wait_ge(sems["at_tanh"], tanh0 + kt - 1)
                                    te.matmul(psS[(sim0 + kt) % 2],
                                              kTb[sl][:, kt * 128:(kt + 1) * 128],
                                              qT[sl][:, qb * 512:(qb + 1) * 512],
                                              start=True, stop=True
                                              ).then_inc(sems["at_sim"], 1)
                                    if kt >= 1:
                                        j = kt - 1
                                        te.wait_ge(sems["at_exp"], exp0 + j + 1)
                                        te.matmul(psO, vvb[sl][:, j * 128:(j + 1) * 128],
                                                  pTb[(exp0 + j) % 4][:],
                                                  start=(j == 0), stop=False)
                                        te.matmul(psR, ones_col[:],
                                                  pTb[(exp0 + j) % 4][:],
                                                  start=(j == 0), stop=False
                                                  ).then_inc(sems["at_av"], 1)
                                j = nkt - 1
                                te.wait_ge(sems["at_exp"], exp0 + j + 1)
                                te.matmul(psO, vvb[sl][:, j * 128:(j + 1) * 128],
                                          pTb[(exp0 + j) % 4][:],
                                          start=(j == 0), stop=True)
                                te.matmul(psR, ones_col[:], pTb[(exp0 + j) % 4][:],
                                          start=(j == 0), stop=True
                                          ).then_inc(sems["at_av"], 1)
                                te.wait_ge(sems["at_scl"], scl_prev + 1)
                                if gqb >= 1:
                                    te.wait_ge(sems["at_bcev"], gqb)
                                te.matmul(psBc, ones_row[:], srow[:],
                                          start=True, stop=True).then_inc(sems["at_bc"], 1)
                            ph.add("tensor", _pe)
                            for kt in range(nkt):
                                inc("at_sim", 1)
                                inc("at_av", 1)
                            inc("at_bc", 1)

                            for kt in range(nkt):
                                rel = kt - 4 * qb
                                masked = rel >= 0
                                gti = tanh0 + kt
                                gei = exp0 + kt

                                def _tanh(sc, kt=kt, sv=sim0 + kt + 1, gti=gti,
                                          expw=exp0 + kt - 1):
                                    sc.wait_ge(sems["at_sim"], sv)
                                    if gti >= 2:
                                        sc.wait_ge(sems["at_exp"], expw)
                                    sc.activation(ttb[gti % 2][:], psS[(sim0 + kt) % 2],
                                                  AF.Tanh, scale=TANH_SCALE
                                                  ).then_inc(sems["at_tanh"], 1)
                                ph.add("scalar", _tanh)
                                tanh_v = inc("at_tanh", 1)

                                if masked:
                                    mi = cnt["at_mask"]
                                    slot_w = (masked_exp_after[mi - 2]
                                              if mi >= 2 else 0)
                                    masked_exp_after.append(gei + 1)

                                    def _mask(ve, kt=kt, rel=rel, wv=tanh_v, mi=mi,
                                              gti=gti, slot_w=slot_w):
                                        ve.wait_ge(sems["at_tanh"], wv)
                                        if slot_w >= 1:
                                            ve.wait_ge(sems["at_exp"], slot_w)
                                        w = (rel + 1) * 128
                                        t1 = ve.tensor_tensor(
                                            tmask[mi % 2][:, 0:w], ttb[gti % 2][:, 0:w],
                                            masks[:, rel * 512:rel * 512 + w], ALU.add)
                                        if w < 512:
                                            t1 = ve.tensor_copy(tmask[mi % 2][:, w:512],
                                                                ttb[gti % 2][:, w:512])
                                        t1.then_inc(sems["at_mask"], 1)
                                    ph.add("vector", _mask)
                                    mask_v = inc("at_mask", 1)
                                else:
                                    mask_v = None

                                def _exp(sc, masked=masked, mask_v=mask_v,
                                         gti=gti, gei=gei, avw=av0 + kt - 3,
                                         mi=(cnt["at_mask"] - 1 if masked else 0)):
                                    if masked:
                                        sc.wait_ge(sems["at_mask"], mask_v)
                                    if gei >= 4:
                                        sc.wait_ge(sems["at_av"], gei - 3)
                                    src = tmask[mi % 2][:] if masked else ttb[gti % 2][:]
                                    sc.activation(pTb[gei % 4][:], src, AF.Exp,
                                                  scale=50.0).then_inc(sems["at_exp"], 1)
                                ph.add("scalar", _exp)
                                inc("at_exp", 1)

                            av_full = cnt["at_av"]

                            def _scl(ve, sl=sl, qb=qb, wv=av_full):
                                ve.wait_ge(sems["at_av"], wv)
                                ve.reciprocal(rrow[:], psR)
                                ve.tensor_tensor(srow[:], rrow[:],
                                                 grow[sl][:, qb * 512:(qb + 1) * 512],
                                                 ALU.mult).then_inc(sems["at_scl"], 1)
                            ph.add("vector", _scl)
                            inc("at_scl", 1)

                            bc_v = cnt["at_bc"]

                            def _bcev(sc, wv=bc_v, gqb=gqb, otw=gqb - 1):
                                sc.wait_ge(sems["at_bc"], wv)
                                if gqb >= 2:
                                    sc.wait_ge(sems["at_ot"], otw)
                                sc.copy(bcsb[gqb % 2][:], psBc).then_inc(sems["at_bcev"], 1)
                            ph.add("scalar", _bcev)
                            bcev_v = inc("at_bcev", 1)

                            def _otev(ve, h=h, qb=qb, wv=bcev_v, gqb=gqb):
                                ve.wait_ge(sems["at_bcev"], wv)
                                ve.tensor_tensor(
                                    oaT[:, h * N + qb * 512:h * N + (qb + 1) * 512],
                                    psO, bcsb[gqb % 2][:], ALU.mult
                                    ).then_inc(sems["at_ot"], 1)
                            ph.add("vector", _otev)
                            inc("at_ot", 1)

                    barrier(ph)
                    ph.flush(block)

                    ph = Phase()
                    b_smm = cnt["s4_mm"]
                    b_sev = cnt["s4_ev"]
                    b_sst = cnt["s4_st"]
                    for tt in range(16):
                        pbank = psB if tt % 2 == 0 else psA

                        def _smm(te, tt=tt, pbank=pbank, wv=wout_v,
                                 evw=b_sev + tt - 1):
                            te.wait_ge(sems["w_ld"], wv)
                            if tt >= 2:
                                te.wait_ge(sems["s4_ev"], evw)
                            last = None
                            for dj in range(4):
                                for hc in range(H):
                                    last = te.matmul(
                                        pbank[:, dj * 512:(dj + 1) * 512],
                                        oaT[:, hc * N + tt * 128:hc * N + (tt + 1) * 128],
                                        wout_sb[:].rearrange("p (hc m) -> p hc m", hc=H)[
                                            :, hc, dj * 512:(dj + 1) * 512],
                                        start=(hc == 0), stop=(hc == H - 1))
                            last.then_inc(sems["s4_mm"], 1)
                        ph.add("tensor", _smm)
                        sm_v = inc("s4_mm", 1)

                        def _sev(sc, tt=tt, pbank=pbank, wv=sm_v,
                                 stw=b_sst + 16 * (tt - 1)):
                            sc.wait_ge(sems["s4_mm"], wv)
                            if tt >= 2:
                                sc.wait_ge(sems["s4_st"], stw)
                            sc.copy(stageB[tt % 2][:], pbank[:]).then_inc(sems["s4_ev"], 1)
                        ph.add("scalar", _sev)
                        se_v = inc("s4_ev", 1)

                        def _sst(sy, b=b, tt=tt, wv=se_v):
                            sy.wait_ge(sems["s4_ev"], wv)
                            sy.dma_start(out=s_s[b, tt * 128:(tt + 1) * 128, :],
                                         in_=stageB[tt % 2][:]).then_inc(sems["s4_st"], 16)
                        ph.add("sync", _sst)
                        inc("s4_st", 16)
                    barrier(ph)
                    ph.flush(block)

        with ExitStack() as sc2:
            e2 = sc2.enter_context
            wtile = [e2(nc.sbuf_tensor(f"wtile{i}", [128, D], F32)) for i in range(2)]
            absp = e2(nc.sbuf_tensor("absp", [128, 128], F32))
            redv = e2(nc.sbuf_tensor("redv", [128, 2], F32))
            onesf = e2(nc.sbuf_tensor("onesf", [128, 1], F32))
            tau11 = e2(nc.sbuf_tensor("tau11", [1, 4], F32))
            tauA = e2(nc.sbuf_tensor("tauA", [1, 2], F32))
            tau128 = e2(nc.sbuf_tensor("tau128", [128, 4], F32))
            terna = e2(nc.sbuf_tensor("terna", [128, D], F32))
            ternb = e2(nc.sbuf_tensor("ternb", [128, D], F32))
            ternt = [e2(nc.sbuf_tensor(f"ternt{i}", [128, D], BF16)) for i in range(2)]
            tstage = [e2(nc.sbuf_tensor(f"tstage{i}", [128, D], BF16)) for i in range(2)]
            psT = [e2(nc.psum_tensor(f"psT{i}", [128, 512], BF16)) for i in range(2)]
            psSc = e2(nc.psum_tensor("psSc", [128, 4], F32))

            ph = Phase()

            def _absload(sy):
                for i in range(128):
                    if i >= 2:
                        sy.wait_ge(sems["t5_abs"], i - 1)
                    if i < 64:
                        src = w1_in[i * 128:(i + 1) * 128, :]
                        sy.dma_start(out=wtile[i % 2][:], in_=src
                                     ).then_inc(sems["t5_ld"], 16)
                    else:
                        j = i - 64
                        dc, c = j // 4, j % 4
                        src = w2_in[dc * 128:(dc + 1) * 128, c * D:(c + 1) * D]
                        sy.dma_start(out=wtile[i % 2][:], in_=src
                                     ).then_inc(sems["t5_ld"], 16)
            ph.add("sync", _absload)
            for i in range(128):
                inc("t5_ld", 16)

            def _absred(ve):
                ve.memset(onesf[:], 1.0)
                for i in range(128):
                    ve.wait_ge(sems["t5_ld"], (i + 1) * 16)
                    ve.tensor_reduce(absp[:, i:i + 1], wtile[i % 2][:], AX.X,
                                     ALU.add, apply_absolute_value=True
                                     ).then_inc(sems["t5_abs"], 1)
                    ve.wait_ge(sems["t5_abs"], i + 1)
                ve.tensor_reduce(redv[:, 0:1], absp[:, 0:64], AX.X, ALU.add
                                 ).then_inc(sems["t5_abs"], 1)
                ve.wait_ge(sems["t5_abs"], 129)
                ve.tensor_reduce(redv[:, 1:2], absp[:, 64:128], AX.X, ALU.add
                                 ).then_inc(sems["t5_abs"], 1)
            ph.add("vector", _absred)
            for i in range(130):
                inc("t5_abs", 1)

            def _taumm(te, wv=cnt["t5_abs"]):
                te.wait_ge(sems["t5_abs"], wv)
                te.matmul(psSc[0:1, 0:1], redv[:, 0:1], onesf[:], start=True, stop=True)
                te.matmul(psSc[0:1, 1:2], redv[:, 1:2], onesf[:], start=True,
                          stop=True).then_inc(sems["t5_tau"], 1)
            ph.add("tensor", _taumm)
            t1v = inc("t5_tau", 1)

            def _tauv(sc, wv=t1v):
                sc.wait_ge(sems["t5_tau"], wv)
                sc.activation(tauA[0:1, :], psSc[0:1, 0:2], AF.Relu,
                              scale=1.0 / (DFF * D), bias=taub[0:1, 0:1])
                sc.activation(tau11[0:1, 0:2], tauA[0:1, :], AF.Identity,
                              scale=0.5, bias=taub[0:1, 1:2])
                sc.activation(tau11[0:1, 2:4], tauA[0:1, :], AF.Identity,
                              scale=-0.5, bias=taub[0:1, 2:3]).then_inc(sems["t5_tau"], 1)
            ph.add("scalar", _tauv)
            t2v = inc("t5_tau", 1)

            def _taubc(te, wv=t2v):
                te.wait_ge(sems["t5_tau"], wv)
                te.matmul(psSc[:, 0:4], ones_row_f[:], tau11[0:1, 0:4],
                          start=True, stop=True).then_inc(sems["t5_tau"], 1)
            ph.add("tensor", _taubc)
            t3v = inc("t5_tau", 1)

            def _tauev(sc, wv=t3v):
                sc.wait_ge(sems["t5_tau"], wv)
                sc.copy(tau128[:], psSc[:, 0:4]).then_inc(sems["t5_tau"], 1)
            ph.add("scalar", _tauev)
            inc("t5_tau", 1)
            tau_ready = cnt["t5_tau"]
            abs_done = cnt["t5_abs"]

            t5_tern_b = cnt["t5_tern"]
            t5_tp_b = cnt["t5_tp"]
            t5_ev_b = cnt["t5_ev"]
            t5_st_b = cnt["t5_st"]
            for mi in range(2):
                for fc in range(FCN):
                    i = mi * FCN + fc

                    def _tld(sy, mi=mi, fc=fc, i=i,
                             tw=t5_tern_b + i - 1, aw=abs_done):
                        if i < 2:
                            sy.wait_ge(sems["t5_abs"], aw)
                        else:
                            sy.wait_ge(sems["t5_tern"], tw)
                        if mi == 0:
                            sy.dma_start(out=wtile[i % 2][:],
                                         in_=w1_in[fc * 128:(fc + 1) * 128, :]
                                         ).then_inc(sems["t5_ld"], 16)
                        else:
                            sy.dma_start(
                                out=wtile[i % 2][:].rearrange("p (dc f) -> p dc f", dc=DC),
                                in_=w2_in[:, fc * 128:(fc + 1) * 128
                                          ].rearrange("(dc p) f -> p dc f", p=128)
                            ).then_inc(sems["t5_ld"], 16)
                    ph.add("sync", _tld)
                    tld_v = inc("t5_ld", 16)

                    def _tern(ve, mi=mi, i=i, wv=tld_v, taw=tau_ready,
                              tpw=t5_tp_b + 4 * (i - 1)):
                        ve.wait_ge(sems["t5_ld"], wv)
                        if i == 0:
                            ve.wait_ge(sems["t5_tau"], taw)
                        if i >= 2:
                            ve.wait_ge(sems["t5_tp"], tpw)
                        ve.tensor_scalar(terna[:], wtile[i % 2][:],
                                         tau128[:, mi:mi + 1], 1.0,
                                         ALU.is_gt, ALU.subtract)
                        ve.tensor_scalar(ternb[:], wtile[i % 2][:],
                                         tau128[:, 2 + mi:3 + mi], None, ALU.is_ge)
                        ve.tensor_tensor(ternt[i % 2][:], terna[:], ternb[:],
                                         ALU.add).then_inc(sems["t5_tern"], 1)
                    ph.add("vector", _tern)
                    tern_v = inc("t5_tern", 1)

                    for g in range(4):
                        gi = i * 4 + g

                        def _ttp(te, i=i, g=g, gi=gi, wv=tern_v,
                                 evw=t5_ev_b + gi - 1):
                            te.wait_ge(sems["t5_tern"], wv)
                            if gi >= 2:
                                te.wait_ge(sems["t5_ev"], evw)
                            last = None
                            for j in range(4):
                                dc = g * 4 + j
                                last = te.transpose(psT[gi % 2][:, j * 128:(j + 1) * 128],
                                                    ternt[i % 2][:, dc * 128:(dc + 1) * 128],
                                                    identb[:])
                            last.then_inc(sems["t5_tp"], 1)
                        ph.add("tensor", _ttp)
                        tp_v = inc("t5_tp", 1)

                        def _tev(sc, i=i, g=g, gi=gi, wv=tp_v,
                                 stw=t5_st_b + 16 * (i - 1)):
                            sc.wait_ge(sems["t5_tp"], wv)
                            if g == 0 and i >= 2:
                                sc.wait_ge(sems["t5_st"], stw)
                            sc.copy(tstage[i % 2][:].rearrange(
                                "p (g4 j t) -> p g4 j t", g4=4, j=4)[:, g],
                                psT[gi % 2][:].rearrange("p (j t) -> p j t", j=4)
                                ).then_inc(sems["t5_ev"], 1)
                        ph.add("scalar", _tev)
                        inc("t5_ev", 1)

                    def _tst(sy, mi=mi, fc=fc, i=i, wv=cnt["t5_ev"]):
                        sy.wait_ge(sems["t5_ev"], wv)
                        if mi == 0:
                            dst = w1qT_s[:, :, fc * 128:(fc + 1) * 128
                                         ].rearrange("dc p f -> p dc f")
                            sy.dma_start(out=dst, in_=tstage[i % 2][:].rearrange(
                                "p (dc f) -> p dc f", dc=DC)).then_inc(sems["t5_st"], 16)
                        else:
                            sy.dma_start(out=w2qT_s[fc * 128:(fc + 1) * 128, :],
                                         in_=tstage[i % 2][:]).then_inc(sems["t5_st"], 16)
                    ph.add("sync", _tst)
                    inc("t5_st", 16)

            barrier(ph)
            ph.flush(block)

        with ExitStack() as sc3:
            e3 = sc3.enter_context
            xnb = e3(nc.sbuf_tensor("xnb", [128, DC * 1024], BF16))
            w1sl = [e3(nc.sbuf_tensor(f"w1sl{i}", [128, DC * 256], BF16))
                    for i in range(2)]
            w2gb = e3(nc.sbuf_tensor("w2gb", [128, 8 * D], BF16))
            hTg = e3(nc.sbuf_tensor("hTg", [128, 8 * 1024], BF16))
            y_acc = e3(nc.sbuf_tensor("y_acc", [128, 8 * D], BF16))
            b2b = e3(nc.sbuf_tensor("b2b", [128, D], BF16))
            yb = e3(nc.sbuf_tensor("yb", [128, D], F32))
            gg = e3(nc.sbuf_tensor("gg", [128, D], BF16))
            sld = [e3(nc.sbuf_tensor(f"sld{i}", [128, D], BF16)) for i in range(2)]
            xld = [e3(nc.sbuf_tensor(f"xld{i}", [128, D], F32)) for i in range(2)]
            ost = [e3(nc.sbuf_tensor(f"ost{i}", [128, D], F32)) for i in range(2)]
            psH = [e3(nc.psum_tensor(f"psH{i}", [128, 1024], F32)) for i in range(2)]
            psY = [e3(nc.psum_tensor(f"psY{i}", [128, 512], F32)) for i in range(2)]

            xnb_view = xnb[:].rearrange("p (dc t) -> p dc t", dc=DC)

            ph = Phase()

            def _b2mm(te):
                for dj in range(4):
                    if dj >= 2:
                        te.wait_ge(sems["f6_yev"], dj - 1)
                    te.matmul(psY[dj % 2][:], ones_row_f[:],
                              b2row[0:1, dj * 512:(dj + 1) * 512],
                              start=True, stop=True).then_inc(sems["f6_y"], 1)
            ph.add("tensor", _b2mm)

            def _b2ev(sc):
                for dj in range(4):
                    sc.wait_ge(sems["f6_y"], dj + 1)
                    sc.copy(b2b[:, dj * 512:(dj + 1) * 512], psY[dj % 2][:]
                            ).then_inc(sems["f6_yev"], 1)
            ph.add("scalar", _b2ev)
            for dj in range(4):
                inc("f6_y", 1)
                inc("f6_yev", 1)
            barrier(ph)
            ph.flush(block)

            NG = 8
            GC = 8
            gsl_ctr = [0]
            for blk in range(4):
                b = blk // 2
                t0 = (blk % 2) * 1024
                ph = Phase()
                h_prev_blk = cnt["f6_h"]

                def _xnbld(sy, b=b, t0=t0, hw=h_prev_blk):
                    if hw >= 1:
                        sy.wait_ge(sems["f6_h"], hw)
                    sy.dma_start(out=xnb_view,
                                 in_=xnT_s[b][:, :, t0:t0 + 1024
                                              ].rearrange("dc p t -> p dc t")
                                 ).then_inc(sems["f6_xnb"], 16)
                ph.add("sync", _xnbld)
                xnb_v = inc("f6_xnb", 16)

                import os as _os
                _nglim = int(_os.environ.get("K_NGLIM", "99"))
                for g in range(NG):
                    if blk == 0 and g >= _nglim:
                        continue
                    y_prev = cnt["f6_y"]

                    for ss in range(4):
                        gsl = gsl_ctr[0]
                        gsl_ctr[0] += 1
                        sg = g * 4 + ss

                        def _w1ld(sy, sg=sg, gsl=gsl, hw=2 * (gsl - 1)):
                            if gsl >= 2:
                                sy.wait_ge(sems["f6_h"], hw)
                            sy.dma_start(
                                out=w1sl[gsl % 2][:].rearrange("p (dc f) -> p dc f", dc=DC),
                                in_=w1qT_s[:, :, sg * 256:(sg + 1) * 256
                                           ].rearrange("dc p f -> p dc f")
                            ).then_inc(sems["f6_w1"], 16)
                        ph.add("sync", _w1ld)
                        inc("f6_w1", 16)
                    w1_vals = [cnt["f6_w1"] - 16 * (3 - s) for s in range(4)]

                    def _w2ld(sc, g=g, yw=y_prev):
                        if yw >= 1:
                            sc.wait_ge(sems["f6_y"], yw)
                        for j in range(GC):
                            fc = g * GC + j
                            sc.dma_start(out=w2gb[:, j * D:(j + 1) * D],
                                         in_=w2qT_s[fc * 128:(fc + 1) * 128, :]
                                         ).then_inc(sems["f6_w2"], 16)
                    ph.add("scalar", _w2ld)
                    inc("f6_w2", 16 * GC)
                    w2_v = cnt["f6_w2"]

                    gelu0 = cnt["f6_gelu"]
                    for j in range(GC):
                        fc = g * GC + j
                        gh = cnt["f6_h"]

                        def _hmm(te, j=j, gh=gh, xv=xnb_v, yw=y_prev,
                                 wv=w1_vals[j // 2], gsl_base=gsl_ctr[0] - 4,
                                 geluw=gh - 1):
                            te.wait_ge(sems["f6_w1"], wv)
                            if j == 0:
                                te.wait_ge(sems["f6_xnb"], xv)
                                if yw >= 1:
                                    te.wait_ge(sems["f6_y"], yw)
                            if gh >= 2:
                                te.wait_ge(sems["f6_gelu"], geluw)
                            last = None
                            for half in range(2):
                                for dc in range(DC):
                                    last = te.matmul(
                                        psH[gh % 2][:, half * 512:(half + 1) * 512],
                                        w1sl[(gsl_base + j // 2) % 2][:].rearrange(
                                            "p (dc f) -> p dc f", dc=DC)[
                                            :, dc, (j % 2) * 128:(j % 2 + 1) * 128],
                                        xnb_view[:, dc, half * 512:(half + 1) * 512],
                                        start=(dc == 0), stop=(dc == DC - 1))
                            last.then_inc(sems["f6_h"], 1)
                        ph.add("tensor", _hmm)
                        h_v = inc("f6_h", 1)

                        def _gelu(sc, j=j, fc=fc, gh=gh, wv=h_v):
                            sc.wait_ge(sems["f6_h"], wv)
                            sc.activation(hTg[:, j * 1024:(j + 1) * 1024],
                                          psH[gh % 2][:], AF.Gelu,
                                          bias=b1_sb[:, fc:fc + 1]
                                          ).then_inc(sems["f6_gelu"], 1)
                        ph.add("scalar", _gelu)
                        inc("f6_gelu", 1)

                    gelu_all = cnt["f6_gelu"]

                    for tt in range(8):
                        for dj in range(4):
                            gy = cnt["f6_y"]

                            def _ymm(te, tt=tt, dj=dj, gy=gy, gw=gelu_all,
                                     w2v=w2_v, yevw=gy - 1,
                                     first=(tt == 0 and dj == 0)):
                                if first:
                                    te.wait_ge(sems["f6_gelu"], gw)
                                    te.wait_ge(sems["f6_w2"], w2v)
                                if gy >= 2:
                                    te.wait_ge(sems["f6_yev"], yevw)
                                last = None
                                for j in range(GC):
                                    last = te.matmul(
                                        psY[gy % 2][:],
                                        hTg[:, j * 1024 + tt * 128:
                                            j * 1024 + (tt + 1) * 128],
                                        w2gb[:, j * D + dj * 512:j * D + (dj + 1) * 512],
                                        start=(j == 0), stop=(j == GC - 1))
                                last.then_inc(sems["f6_y"], 1)
                            ph.add("tensor", _ymm)
                            y_v = inc("f6_y", 1)

                            def _yev(ve, g=g, tt=tt, dj=dj, gy=gy, wv=y_v,
                                     siluw=cnt["f6_silu"]):
                                ve.wait_ge(sems["f6_y"], wv)
                                ya = y_acc[:, tt * D + dj * 512:tt * D + (dj + 1) * 512]
                                if g == 0:
                                    ve.tensor_tensor(ya, psY[gy % 2][:],
                                                     b2b[:, dj * 512:(dj + 1) * 512],
                                                     ALU.add).then_inc(sems["f6_yev"], 1)
                                elif g < NG - 1:
                                    ve.tensor_tensor(ya, ya, psY[gy % 2][:], ALU.add
                                                     ).then_inc(sems["f6_yev"], 1)
                                else:
                                    if siluw >= 1 and dj == 0:
                                        ve.wait_ge(sems["f6_silu"], siluw)
                                    ve.tensor_tensor(yb[:, dj * 512:(dj + 1) * 512],
                                                     ya, psY[gy % 2][:], ALU.add
                                                     ).then_inc(sems["f6_yev"], 1)
                            ph.add("vector", _yev)
                            inc("f6_yev", 1)

                        if g == NG - 1 and _os.environ.get("K_NOCMB", "0") != "1":
                            gtt = blk * 8 + tt

                            def _sxld(sy, b=b, t0=t0, tt=tt, gtt=gtt,
                                      ow=16 * (gtt - 1)):
                                if gtt >= 2:
                                    sy.wait_ge(sems["f6_out"], ow)
                                r0 = t0 + tt * 128
                                if _op_real("xld"):
                                    sy.dma_start(out=xld[gtt % 2][:],
                                                 in_=x_in[b, r0:r0 + 128, :]
                                                 ).then_inc(sems["f6_xld"], 16)
                                else:
                                    sy.sem_inc(sems["f6_xld"], 16)
                            ph.add("sync", _sxld)
                            xld_v = inc("f6_xld", 16)

                            def _sldf(sc, b=b, t0=t0, tt=tt, gtt=gtt,
                                      ow=16 * (gtt - 1)):
                                if gtt >= 2:
                                    sc.wait_ge(sems["f6_out"], ow)
                                r0 = t0 + tt * 128
                                if _op_real("sld"):
                                    sc.dma_start(out=sld[gtt % 2][:],
                                                 in_=s_s[b, r0:r0 + 128, :]
                                                 ).then_inc(sems["f6_sld"], 16)
                                else:
                                    sc.sem_inc(sems["f6_sld"], 16)
                            if _os.environ.get("K_CMBV", "0") == "1":
                                ph.add("sync", _sldf)
                            else:
                                ph.add("scalar", _sldf)
                            sld_v = inc("f6_sld", 16)

                            yev_all = cnt["f6_yev"]

                            def _silu(sc, wv=yev_all, cw=cnt["f6_cmb"]):
                                sc.wait_ge(sems["f6_yev"], wv)
                                if cw >= 1:
                                    sc.wait_ge(sems["f6_cmb"], cw)
                                if _op_real("silu"):
                                    sc.activation(gg[:], yb[:], AF.Silu
                                                  ).then_inc(sems["f6_silu"], 1)
                                elif _op_real("silucp"):
                                    sc.copy(gg[:], yb[:]).then_inc(sems["f6_silu"], 1)
                                elif _op_real("silujunk"):
                                    sc.copy(ost[0][:], xnb[:, 0:D]).then_inc(sems["f6_silu"], 1)
                                elif _op_real("siluyb"):
                                    sc.copy(ost[0][:], yb[:]).then_inc(sems["f6_silu"], 1)
                                elif _op_real("silugg"):
                                    sc.copy(gg[:], xnb[:, 0:D]).then_inc(sems["f6_silu"], 1)
                                else:
                                    sc.sem_inc(sems["f6_silu"], 1)
                            ph.add("scalar", _silu)
                            silu_v = inc("f6_silu", 1)

                            def _cmb(ve, gtt=gtt, wv=silu_v, sv=sld_v, xv=xld_v,
                                     ow=16 * (gtt - 1)):
                                ve.wait_ge(sems["f6_silu"], wv)
                                ve.wait_ge(sems["f6_sld"], sv)
                                ve.wait_ge(sems["f6_xld"], xv)
                                if gtt >= 2:
                                    ve.wait_ge(sems["f6_out"], ow)
                                if _op_real("cmb"):
                                    ve.tensor_tensor(gg[:], gg[:], sld[gtt % 2][:],
                                                     ALU.mult)
                                    ve.tensor_tensor(ost[gtt % 2][:], gg[:],
                                                     xld[gtt % 2][:], ALU.add
                                                     ).then_inc(sems["f6_cmb"], 1)
                                elif _op_real("cmbcp"):
                                    ve.tensor_copy(ost[gtt % 2][:], xld[gtt % 2][:]
                                                   ).then_inc(sems["f6_cmb"], 1)
                                else:
                                    ve.sem_inc(sems["f6_cmb"], 1)
                            ph.add("vector", _cmb)
                            cmb_v = inc("f6_cmb", 1)

                            def _ost(sy, b=b, t0=t0, tt=tt, gtt=gtt, wv=cmb_v):
                                sy.wait_ge(sems["f6_cmb"], wv)
                                r0 = t0 + tt * 128
                                if _op_real("ost"):
                                    sy.dma_start(out=out[b, r0:r0 + 128, :],
                                                 in_=ost[gtt % 2][:]
                                                 ).then_inc(sems["f6_out"], 16)
                                else:
                                    sy.sem_inc(sems["f6_out"], 16)
                            ph.add("sync", _ost)
                            inc("f6_out", 16)

                barrier(ph)
                ph.flush(block)

            ph = Phase()
            out_total = cnt["f6_out"]

            def _fin(sy, wv=out_total):
                sy.wait_ge(sems["f6_out"], wv)
            ph.add("sync", _fin)
            ph.flush(block)

    return nc


_CACHED = {}


def kernel(L0_attn_x, L0_attn_w_qkv, L0_attn_w_gates, L0_attn_w_out,
           L0_ffn_w1, L0_ffn_b1, L0_ffn_w2, L0_ffn_b2):
    x = np.ascontiguousarray(np.asarray(L0_attn_x, dtype=np.float32))
    wqkv_a = np.ascontiguousarray(np.asarray(L0_attn_w_qkv, dtype=np.float32))
    wg_a = np.ascontiguousarray(np.asarray(L0_attn_w_gates, dtype=np.float32))
    wout_a = np.ascontiguousarray(np.asarray(L0_attn_w_out, dtype=np.float32))
    w1_a = np.ascontiguousarray(np.asarray(L0_ffn_w1, dtype=np.float32))
    b1_a = np.ascontiguousarray(np.asarray(L0_ffn_b1, dtype=np.float32))
    w2_a = np.ascontiguousarray(np.asarray(L0_ffn_w2, dtype=np.float32))
    b2_a = np.ascontiguousarray(np.asarray(L0_ffn_b2, dtype=np.float32))

    if "nc" not in _CACHED:
        _CACHED["nc"] = build()
    nc = _CACHED["nc"]

    in_maps = [{"x": x[c * B:(c + 1) * B], "wqkv": wqkv_a, "wg": wg_a,
                "wout": wout_a, "w1": w1_a, "b1": b1_a, "w2": w2_a, "b2": b2_a}
               for c in range(NCORES)]
    res = run_bass_kernel_spmd(nc, in_maps, core_ids=list(range(NCORES)))
    return np.concatenate([r["out"] for r in res.results], axis=0)


# version 02
# speedup vs baseline: 1.1734x; 1.1734x over previous
import numpy as np
import ml_dtypes
import concourse.bass as bass
import concourse.mybir as mybir
from concourse.bass_utils import run_bass_kernel_spmd
from contextlib import ExitStack

F32 = mybir.dt.float32
BF16 = mybir.dt.bfloat16
AF = mybir.ActivationFunctionType
ALU = mybir.AluOpType
AX = mybir.AxisListType

NCORES = 8
B = 2
N = 2048
D = 2048
H = 16
DH = 128
DFF = 8192
DC = D // 128
FCN = DFF // 128
EPS = 1e-5
TANH_SCALE = (DH ** -0.5) / 50.0
NEG = -1.0e9

ENGS = ("sync", "tensor", "vector", "scalar", "gpsimd")


_PHASE_N = [0]

def _op_real(tok):
    import os
    v = os.environ.get("K_OPS")
    if v is None:
        return True
    return tok in v.split(",")



class Phase:
    def __init__(self):
        self.ops = {e: [] for e in ENGS}

    def add(self, eng, fn):
        self.ops[eng].append(fn)

    def flush(self, block):
        import os
        _PHASE_N[0] += 1
        limit = int(os.environ.get("K_UPTO", "1000000"))
        if _PHASE_N[0] > limit:
            self.ops = {e: [] for e in ENGS}
            return
        for e in ENGS:
            fns = self.ops[e]
            if fns:
                getattr(block, e)(lambda en, fns=fns: [f(en) for f in fns])
        self.ops = {e: [] for e in ENGS}


def build(dump=False):
    nc = bass.Bass("TRN2", target_bir_lowering=False)

    x_in = nc.dram_tensor("x", [B, N, D], F32, kind="ExternalInput").ap()
    wqkv = nc.dram_tensor("wqkv", [D, 3 * D], F32, kind="ExternalInput").ap()
    wg_in = nc.dram_tensor("wg", [D, H], F32, kind="ExternalInput").ap()
    wout_in = nc.dram_tensor("wout", [D, D], F32, kind="ExternalInput").ap()
    w1_in = nc.dram_tensor("w1", [DFF, D], F32, kind="ExternalInput").ap()
    b1_in = nc.dram_tensor("b1", [DFF], F32, kind="ExternalInput").ap()
    w2_in = nc.dram_tensor("w2", [D, DFF], F32, kind="ExternalInput").ap()
    b2_in = nc.dram_tensor("b2", [D], F32, kind="ExternalInput").ap()
    out = nc.dram_tensor("out", [B, N, D], F32, kind="ExternalOutput").ap()

    sk = "ExternalOutput" if dump else "Internal"
    xnT_s = nc.dram_tensor("xnT_s", [B, DC, 128, N], BF16, kind=sk).ap()
    qkT_s = nc.dram_tensor("qkT_s", [B, H, 2, 128, N], BF16, kind=sk).ap()
    v_s = nc.dram_tensor("v_s", [B, N, D], BF16, kind=sk).ap()
    gates_s = nc.dram_tensor("gates_s", [H, B * N], BF16, kind=sk).ap()
    s_s = nc.dram_tensor("s_s", [B, N, D], BF16, kind=sk).ap()
    w1qT_s = nc.dram_tensor("w1qT_s", [DC, 128, DFF], BF16, kind=sk).ap()
    w2qT_s = nc.dram_tensor("w2qT_s", [DFF, D], BF16, kind=sk).ap()

    identf_c = nc.inline_tensor(np.eye(128, dtype=np.float32), "identf").ap()
    identb_c = nc.inline_tensor(np.eye(128, dtype=ml_dtypes.bfloat16), "identb").ap()
    ones_row_c = nc.inline_tensor(np.ones((1, 128), dtype=ml_dtypes.bfloat16), "onesrow").ap()
    ones_col_c = nc.inline_tensor(np.ones((128, 1), dtype=ml_dtypes.bfloat16), "onescol").ap()
    ones_row_f_c = nc.inline_tensor(np.ones((1, 128), dtype=np.float32), "onesrowf").ap()
    kkn = np.arange(128)[:, None]
    qqn = np.arange(512)[None, :]
    masks_np = np.stack([np.where(kkn + r * 128 > qqn, NEG, 0.0).astype(np.float32)
                         for r in range(4)])
    masks_c = nc.inline_tensor(masks_np, "masks").ap()
    taubias_c = nc.inline_tensor(np.array([[-1e-8, 0.5e-8, -0.5e-8]], dtype=np.float32),
                                 "taubias").ap()

    sems = {}
    cnt = {}

    def inc(name, k=1):
        cnt[name] += k
        return cnt[name]

    with ExitStack() as stack:
        ee = stack.enter_context

        for name in ("bar", "c_ld",
                     "x_ld", "ln_st", "ln_bn", "ln_norm", "ln_tp", "ln_ev", "xnT_st",
                     "w_ld", "qk_mm", "qk_ev", "qk_st", "g_mm",
                     "v_mm", "v_ev", "v_st",
                     "at_ld", "at_sim", "at_tanh", "at_mask", "at_exp",
                     "at_av", "at_scl", "at_bc", "at_bcev", "at_ot",
                     "s4_mm", "s4_ev", "s4_st",
                     "t5_ld", "t5_ld2", "t5_abs", "t5_tau", "t5_tern", "t5_tp",
                     "t5_ev", "t5_st",
                     "f6_xnb", "f6_w1", "f6_w2", "f6_h", "f6_gelu",
                     "f6_y", "f6_yev", "f6_silu", "f6_sld", "f6_xld",
                     "f6_cmb", "f6_out"):
            sems[name] = ee(nc.semaphore(name))
            cnt[name] = 0

        def barrier(ph):
            tgt = cnt["bar"] + 5
            cnt["bar"] = tgt
            hh = sems["bar"]
            for e in ENGS:
                ph.add(e, lambda en, t=tgt: (en.sem_inc(hh, 1), en.wait_ge(hh, t)))

        identf = ee(nc.sbuf_tensor("identf_b", [128, 128], F32))
        identb = ee(nc.sbuf_tensor("identb_b", [128, 128], BF16))
        ones_row = ee(nc.sbuf_tensor("ones_row_b", [1, 128], BF16))
        ones_col = ee(nc.sbuf_tensor("ones_col_b", [128, 1], BF16))
        ones_row_f = ee(nc.sbuf_tensor("ones_row_f_b", [1, 128], F32))
        masks = ee(nc.sbuf_tensor("masks_b", [128, 4 * 512], F32))
        b1_sb = ee(nc.sbuf_tensor("b1_sb", [128, FCN], F32))
        b2row = ee(nc.sbuf_tensor("b2row_b", [1, D], F32))
        taub = ee(nc.sbuf_tensor("taub_b", [1, 3], F32))

        block = ee(nc.Block())

        ph = Phase()

        def _setup(sy):
            for dst, src in ((identf[:], identf_c), (identb[:], identb_c),
                             (ones_row[:], ones_row_c), (ones_col[:], ones_col_c),
                             (ones_row_f[:], ones_row_f_c),
                             (b2row[:], b2_in.rearrange("(one d) -> one d", one=1)),
                             (taub[:], taubias_c)):
                sy.dma_start(out=dst, in_=src).then_inc(sems["c_ld"], 16)
            sy.dma_start(out=masks[:].rearrange("p (r q) -> p r q", r=4),
                         in_=masks_c.rearrange("r p q -> p r q")).then_inc(sems["c_ld"], 16)
        ph.add("sync", _setup)
        inc("c_ld", 8 * 16)

        def _setup_gp(gp):
            gp.dma_start(out=b1_sb[:], in_=b1_in.rearrange("(fc p) -> p fc", p=128),
                         allow_slow_non_contiguous=True).then_inc(sems["c_ld"], 16)
        ph.add("gpsimd", _setup_gp)
        inc("c_ld", 16)
        barrier(ph)
        ph.flush(block)

        with ExitStack() as scps:
            psA = scps.enter_context(nc.psum_tensor("psA", [128, 2048], F32))
            psB = scps.enter_context(nc.psum_tensor("psB", [128, 2048], F32))
            for b in range(B):
                with ExitStack() as sca:
                    ea = sca.enter_context
                    xnT_sb = ea(nc.sbuf_tensor(f"xnT_sb{b}", [128, DC * N], BF16))
                    wbig = ea(nc.sbuf_tensor(f"wv_sb{b}", [128, DC * D], BF16))
                    wq4 = [ea(nc.sbuf_tensor(f"wq4_{b}_{i}", [128, DC * 256], BF16))
                           for i in range(2)]
                    xf = [ea(nc.sbuf_tensor(f"xf{b}_{i}", [128, D], F32)) for i in range(2)]
                    xn_t = [ea(nc.sbuf_tensor(f"xnt{b}_0", [128, D], F32))]
                    bnst = ea(nc.sbuf_tensor(f"bnst{b}", [128, 24], F32))
                    bnmv = ea(nc.sbuf_tensor(f"bnmv{b}", [128, 2], F32))
                    veps = ea(nc.sbuf_tensor(f"veps{b}", [128, 1], F32))
                    sdv = ea(nc.sbuf_tensor(f"sdv{b}", [128, 1], F32))
                    rinv = ea(nc.sbuf_tensor(f"rinv{b}", [128, 1], F32))
                    stage = [ea(nc.sbuf_tensor(f"stageA{b}_{i}", [128, D], BF16))
                             for i in range(2)]
                    gstage = ea(nc.sbuf_tensor(f"gstage{b}", [16, N], BF16))
                    wgsb = ea(nc.sbuf_tensor(f"wgsb{b}", [128, DC * 16], BF16))

                    xnT_view = xnT_sb[:].rearrange("p (dc t) -> p dc t", dc=DC)

                    ph = Phase()
                    b_norm = cnt["ln_norm"]
                    b_tp = cnt["ln_tp"]
                    b_ev = cnt["ln_ev"]
                    b_st = cnt["ln_st"]
                    b_xld = cnt["x_ld"]
                    for tt in range(16):
                        sl = tt % 2

                        def _ld(sy, b=b, tt=tt, sl=sl, nw=b_norm + tt - 1):
                            if tt >= 2:
                                sy.wait_ge(sems["ln_norm"], nw)
                            sy.dma_start(out=xf[sl][:],
                                         in_=x_in[b, tt * 128:(tt + 1) * 128, :]
                                         ).then_inc(sems["x_ld"], 16)
                        ph.add("sync", _ld)
                        x_ld_v = inc("x_ld", 16)

                        bnb = cnt["ln_bn"]

                        def _ln(ve, sl=sl, wv=x_ld_v, bnb=bnb):
                            ve.wait_ge(sems["x_ld"], wv)
                            k = bnb
                            for c in range(4):
                                ve.bn_stats(bnst[:, c * 6:(c + 1) * 6],
                                            xf[sl][:, c * 512:(c + 1) * 512]
                                            ).then_inc(sems["ln_bn"], 1)
                                k += 1
                                ve.wait_ge(sems["ln_bn"], k)
                            ve.bn_aggr(bnmv[:], bnst[:]).then_inc(sems["ln_bn"], 1)
                            k += 1
                            ve.wait_ge(sems["ln_bn"], k)
                            ve.tensor_scalar_add(veps[:], bnmv[:, 1:2], EPS
                                                 ).then_inc(sems["ln_st"], 1)
                        ph.add("vector", _ln)
                        inc("ln_bn", 5)
                        v1 = inc("ln_st", 1)

                        def _sq(sc, wv=v1):
                            sc.wait_ge(sems["ln_st"], wv)
                            sc.sqrt(sdv[:], veps[:]).then_inc(sems["ln_st"], 1)
                        ph.add("scalar", _sq)
                        v2 = inc("ln_st", 1)

                        def _nrm(ve, sl=sl, wv=v2, tpw=b_tp + 4 * tt, tt=tt):
                            ve.wait_ge(sems["ln_st"], wv)
                            ve.reciprocal(rinv[:], sdv[:])
                            if tt >= 1:
                                ve.wait_ge(sems["ln_tp"], tpw)
                            ve.tensor_scalar(xn_t[0][:], xf[sl][:], bnmv[:, 0:1],
                                             rinv[:], ALU.subtract, ALU.mult
                                             ).then_inc(sems["ln_norm"], 1)
                        ph.add("vector", _nrm)
                        norm_v = inc("ln_norm", 1)

                        for g in range(4):
                            gi = tt * 4 + g
                            pbank = (psA if gi % 2 == 0 else psB)

                            def _tp(te, g=g, gi=gi, pbank=pbank, wv=norm_v,
                                    evw=b_ev + gi - 1):
                                te.wait_ge(sems["ln_norm"], wv)
                                if gi >= 2:
                                    te.wait_ge(sems["ln_ev"], evw)
                                last = None
                                for j in range(4):
                                    dc = g * 4 + j
                                    last = te.transpose(pbank[:, j * 128:(j + 1) * 128],
                                                        xn_t[0][:, dc * 128:(dc + 1) * 128],
                                                        identf[:])
                                last.then_inc(sems["ln_tp"], 1)
                            ph.add("tensor", _tp)
                            tp_v = inc("ln_tp", 1)

                            def _ev(sc, g=g, tt=tt, pbank=pbank, wv=tp_v):
                                sc.wait_ge(sems["ln_tp"], wv)
                                dst = xnT_view[:, g * 4:(g + 1) * 4, tt * 128:(tt + 1) * 128]
                                src = pbank[:, 0:512].rearrange("p (j t) -> p j t", j=4)
                                sc.copy(dst, src).then_inc(sems["ln_ev"], 1)
                            ph.add("scalar", _ev)
                            inc("ln_ev", 1)

                    def _xst(sy, b=b, wv=cnt["ln_ev"]):
                        sy.wait_ge(sems["ln_ev"], wv)
                        sy.dma_start(out=xnT_s[b].rearrange("dc p t -> p dc t"),
                                     in_=xnT_view).then_inc(sems["xnT_st"], 16)
                    ph.add("sync", _xst)
                    inc("xnT_st", 16)
                    barrier(ph)
                    ph.flush(block)

                    ph = Phase()
                    b_qmm = cnt["qk_mm"]
                    b_qev = cnt["qk_ev"]
                    b_qst = cnt["qk_st"]
                    b_wld = cnt["w_ld"]
                    for qk in range(2):
                        for qu in range(8):
                            li = qk * 8 + qu

                            def _wld(gp, qk=qk, qu=qu, li=li,
                                     mmw=b_qmm + 2 * (li - 1)):
                                if li >= 2:
                                    gp.wait_ge(sems["qk_mm"], mmw)
                                gp.dma_start(
                                    out=wq4[li % 2][:].rearrange("p (dc m) -> p dc m", dc=DC),
                                    in_=wqkv[:, qk * D + qu * 256: qk * D + (qu + 1) * 256
                                             ].rearrange("(dc p) m -> p dc m", p=128)
                                ).then_inc(sems["w_ld"], 16)
                            ph.add("gpsimd", _wld)
                            w_ld_v = inc("w_ld", 16)

                            for hs in range(2):
                                h = qu * 2 + hs
                                g_idx = qk * 16 + h
                                pbank = psA if g_idx % 2 == 0 else psB

                                def _mm(te, li=li, hs=hs, pbank=pbank, wv=w_ld_v,
                                        evw=b_qev + g_idx - 1, g_idx=g_idx):
                                    te.wait_ge(sems["w_ld"], wv)
                                    if g_idx >= 2:
                                        te.wait_ge(sems["qk_ev"], evw)
                                    last = None
                                    for dc in range(DC):
                                        lhs = wq4[li % 2][:].rearrange(
                                            "p (dc m) -> p dc m", dc=DC)[
                                            :, dc, hs * 128:(hs + 1) * 128]
                                        for tq in range(4):
                                            last = te.matmul(
                                                pbank[:, tq * 512:(tq + 1) * 512], lhs,
                                                xnT_view[:, dc, tq * 512:(tq + 1) * 512],
                                                start=(dc == 0), stop=(dc == DC - 1))
                                    last.then_inc(sems["qk_mm"], 1)
                                ph.add("tensor", _mm)
                                mm_v = inc("qk_mm", 1)

                                def _ev(sc, g_idx=g_idx, pbank=pbank, wv=mm_v,
                                        stw=b_qst + 16 * (g_idx - 1)):
                                    sc.wait_ge(sems["qk_mm"], wv)
                                    if g_idx >= 2:
                                        sc.wait_ge(sems["qk_st"], stw)
                                    sc.copy(stage[g_idx % 2][:], pbank[:]
                                            ).then_inc(sems["qk_ev"], 1)
                                ph.add("scalar", _ev)
                                ev_v = inc("qk_ev", 1)

                                def _st(sy, b=b, h=h, qk=qk, g_idx=g_idx, wv=ev_v):
                                    sy.wait_ge(sems["qk_ev"], wv)
                                    sy.dma_start(out=qkT_s[b, h, qk],
                                                 in_=stage[g_idx % 2][:]
                                                 ).then_inc(sems["qk_st"], 16)
                                ph.add("sync", _st)
                                inc("qk_st", 16)

                    def _wgld(gp):
                        gp.dma_start(out=wgsb[:].rearrange("p (dc g) -> p dc g", g=16),
                                     in_=wg_in.rearrange("(dc p) g -> p dc g", p=128)
                                     ).then_inc(sems["w_ld"], 16)
                    ph.add("gpsimd", _wgld)
                    wg_v = inc("w_ld", 16)

                    def _gmm(te, wv=wg_v, evw=cnt["qk_ev"]):
                        te.wait_ge(sems["w_ld"], wv)
                        te.wait_ge(sems["qk_ev"], evw)
                        last = None
                        for tq in range(4):
                            for dc in range(DC):
                                last = te.matmul(
                                    psA[0:16, tq * 512:(tq + 1) * 512],
                                    wgsb[:].rearrange("p (dc g) -> p dc g", g=16)[:, dc, :],
                                    xnT_view[:, dc, tq * 512:(tq + 1) * 512],
                                    start=(dc == 0), stop=(dc == DC - 1))
                        last.then_inc(sems["g_mm"], 1)
                    ph.add("tensor", _gmm)
                    g1 = inc("g_mm", 1)

                    def _gsig(sc, wv=g1):
                        sc.wait_ge(sems["g_mm"], wv)
                        sc.activation(gstage[:], psA[0:16, :], AF.Sigmoid
                                      ).then_inc(sems["g_mm"], 1)
                    ph.add("scalar", _gsig)
                    g2 = inc("g_mm", 1)

                    def _gst(sy, b=b, wv=g2):
                        sy.wait_ge(sems["g_mm"], wv)
                        sy.dma_start(out=gates_s[:, b * N:(b + 1) * N],
                                     in_=gstage[:]).then_inc(sems["qk_st"], 16)
                    ph.add("sync", _gst)
                    inc("qk_st", 16)
                    barrier(ph)
                    ph.flush(block)

                    ph = Phase()
                    b_vmm = cnt["v_mm"]
                    b_vev = cnt["v_ev"]
                    b_vst = cnt["v_st"]

                    def _wvld(gp):
                        gp.dma_start(out=wbig[:].rearrange("p (dc m) -> p dc m", dc=DC),
                                     in_=wqkv[:, 2 * D:3 * D].rearrange(
                                         "(dc p) m -> p dc m", p=128)
                                     ).then_inc(sems["w_ld"], 16)
                    ph.add("gpsimd", _wvld)
                    wv_v = inc("w_ld", 16)

                    for tt in range(16):
                        pbank = psA if tt % 2 == 0 else psB

                        def _vmm(te, tt=tt, pbank=pbank, wv=wv_v, evw=b_vev + tt - 1):
                            te.wait_ge(sems["w_ld"], wv)
                            if tt >= 2:
                                te.wait_ge(sems["v_ev"], evw)
                            last = None
                            for dj in range(4):
                                for dc in range(DC):
                                    last = te.matmul(
                                        pbank[:, dj * 512:(dj + 1) * 512],
                                        xnT_view[:, dc, tt * 128:(tt + 1) * 128],
                                        wbig[:].rearrange("p (dc m) -> p dc m", dc=DC)[
                                            :, dc, dj * 512:(dj + 1) * 512],
                                        start=(dc == 0), stop=(dc == DC - 1))
                            last.then_inc(sems["v_mm"], 1)
                        ph.add("tensor", _vmm)
                        vm_v = inc("v_mm", 1)

                        def _vev(sc, tt=tt, pbank=pbank, wv=vm_v,
                                 stw=b_vst + 16 * (tt - 1)):
                            sc.wait_ge(sems["v_mm"], wv)
                            if tt >= 2:
                                sc.wait_ge(sems["v_st"], stw)
                            sc.copy(stage[tt % 2][:], pbank[:]).then_inc(sems["v_ev"], 1)
                        ph.add("scalar", _vev)
                        ve_v = inc("v_ev", 1)

                        def _vst(sy, b=b, tt=tt, wv=ve_v):
                            sy.wait_ge(sems["v_ev"], wv)
                            sy.dma_start(out=v_s[b, tt * 128:(tt + 1) * 128, :],
                                         in_=stage[tt % 2][:]).then_inc(sems["v_st"], 16)
                        ph.add("sync", _vst)
                        inc("v_st", 16)
                    barrier(ph)
                    ph.flush(block)

                with ExitStack() as scb:
                    eb = scb.enter_context
                    oaT = eb(nc.sbuf_tensor(f"oaT{b}", [128, H * N], BF16))
                    qT = [eb(nc.sbuf_tensor(f"qT{b}_{i}", [128, N], BF16)) for i in range(2)]
                    kTb = [eb(nc.sbuf_tensor(f"kT{b}_{i}", [128, N], BF16)) for i in range(2)]
                    vvb = [eb(nc.sbuf_tensor(f"vv{b}_{i}", [128, N], BF16)) for i in range(2)]
                    grow = [eb(nc.sbuf_tensor(f"grow{b}_{i}", [1, N], BF16)) for i in range(2)]
                    ttg = [eb(nc.sbuf_tensor(f"ttg{b}_{i}", [128, 1024], F32)) for i in range(2)]
                    tmg = [eb(nc.sbuf_tensor(f"tmg{b}_{i}", [128, 1024], F32)) for i in range(2)]
                    pTg = [eb(nc.sbuf_tensor(f"pTg{b}_{i}", [128, 1024], BF16)) for i in range(2)]
                    rrow = eb(nc.sbuf_tensor(f"rrow{b}", [1, 512], F32))
                    srow = eb(nc.sbuf_tensor(f"srow{b}", [1, 512], BF16))
                    bcsb = [eb(nc.sbuf_tensor(f"bcsb{b}_{i}", [128, 512], BF16))
                            for i in range(2)]
                    woutd = [eb(nc.sbuf_tensor(f"woutd{b}_{i}", [128, H * 512], BF16))
                             for i in range(2)]
                    stageB = [eb(nc.sbuf_tensor(f"stageB{b}_{i}", [128, 512], BF16))
                              for i in range(2)]

                    psS = [psA[:, 0:1024], psA[:, 1024:2048]]
                    psBc = psB[:, 0:512]
                    psO = psB[:, 512:1024]
                    psR = psB[0:1, 1024:1536]

                    ph = Phase()
                    if b == 0:
                        masked_exp_after = []

                    for h in range(H):
                        sl = h % 2

                        def _atld(sc, b=b, h=h, sl=sl, otw=cnt["at_ot"] - 4):
                            if h >= 2 or b > 0:
                                sc.wait_ge(sems["at_ot"], otw)
                            sc.dma_start(out=qT[sl][:], in_=qkT_s[b, h, 0]
                                         ).then_inc(sems["at_ld"], 16)
                            sc.dma_start(out=kTb[sl][:], in_=qkT_s[b, h, 1]
                                         ).then_inc(sems["at_ld"], 16)
                            sc.dma_start(out=vvb[sl][:].rearrange("p (kt j) -> p kt j", kt=16),
                                         in_=v_s[b, :, h * 128:(h + 1) * 128
                                                 ].rearrange("(kt p) j -> p kt j", p=128)
                                         ).then_inc(sems["at_ld"], 16)
                            sc.dma_start(out=grow[sl][:],
                                         in_=gates_s[h:h + 1, b * N:(b + 1) * N]
                                         ).then_inc(sems["at_ld"], 16)
                        ph.add("scalar", _atld)
                        at_ld_v = inc("at_ld", 64)

                        for qb in range(4):
                            nkt = 4 * (qb + 1)
                            ngrp = nkt // 2
                            gqb = cnt["at_ot"]
                            sim0 = cnt["at_sim"]
                            tanh0 = cnt["at_tanh"]
                            exp0 = cnt["at_exp"]
                            av0 = cnt["at_av"]
                            scl_prev = cnt["at_scl"]

                            def _pe(te, sl=sl, qb=qb, nkt=nkt, ngrp=ngrp,
                                    ldv=at_ld_v, sim0=sim0, tanh0=tanh0,
                                    exp0=exp0, gqb=gqb, scl_prev=scl_prev):
                                te.wait_ge(sems["at_ld"], ldv)
                                if gqb >= 1:
                                    te.wait_ge(sems["at_ot"], gqb)
                                    te.wait_ge(sems["at_scl"], scl_prev)

                                def av_rs(g):
                                    j0, j1 = 2 * g, 2 * g + 1
                                    pt = pTg[(exp0 + g) % 2]
                                    te.wait_ge(sems["at_exp"], exp0 + g + 1)
                                    te.matmul(psO, vvb[sl][:, j0 * 128:(j0 + 1) * 128],
                                              pt[:, 0:512], start=(j0 == 0), stop=False)
                                    te.matmul(psO, vvb[sl][:, j1 * 128:(j1 + 1) * 128],
                                              pt[:, 512:1024], start=False,
                                              stop=(j1 == nkt - 1))
                                    te.matmul(psR, ones_col[:], pt[:, 0:512],
                                              start=(j0 == 0), stop=False)
                                    te.matmul(psR, ones_col[:], pt[:, 512:1024],
                                              start=False, stop=(j1 == nkt - 1)
                                              ).then_inc(sems["at_av"], 1)

                                for g in range(ngrp):
                                    gsi = sim0 + g
                                    if gsi >= 2:
                                        te.wait_ge(sems["at_tanh"], gsi - 1)
                                    te.matmul(psS[gsi % 2][:, 0:512],
                                              kTb[sl][:, (2 * g) * 128:(2 * g + 1) * 128],
                                              qT[sl][:, qb * 512:(qb + 1) * 512],
                                              start=True, stop=True)
                                    te.matmul(psS[gsi % 2][:, 512:1024],
                                              kTb[sl][:, (2 * g + 1) * 128:(2 * g + 2) * 128],
                                              qT[sl][:, qb * 512:(qb + 1) * 512],
                                              start=True, stop=True
                                              ).then_inc(sems["at_sim"], 1)
                                    if g >= 1:
                                        av_rs(g - 1)
                                av_rs(ngrp - 1)
                                te.wait_ge(sems["at_scl"], scl_prev + 1)
                                if gqb >= 1:
                                    te.wait_ge(sems["at_bcev"], gqb)
                                te.matmul(psBc, ones_row[:], srow[:],
                                          start=True, stop=True).then_inc(sems["at_bc"], 1)
                            ph.add("tensor", _pe)
                            for g in range(ngrp):
                                inc("at_sim", 1)
                                inc("at_av", 1)
                            inc("at_bc", 1)

                            for g in range(ngrp):
                                rel0 = 2 * g - 4 * qb
                                masked = rel0 >= 0
                                gti = tanh0 + g
                                gei = exp0 + g

                                def _tanh(sc, g=g, sv=sim0 + g + 1, gti=gti,
                                          expw=exp0 + g - 1):
                                    sc.wait_ge(sems["at_sim"], sv)
                                    if gti >= 2:
                                        sc.wait_ge(sems["at_exp"], expw)
                                    sc.activation(ttg[gti % 2][:], psS[(sim0 + g) % 2],
                                                  AF.Tanh, scale=TANH_SCALE
                                                  ).then_inc(sems["at_tanh"], 1)
                                ph.add("scalar", _tanh)
                                tanh_v = inc("at_tanh", 1)

                                if masked:
                                    mi = cnt["at_mask"]
                                    slot_w = (masked_exp_after[mi - 2]
                                              if mi >= 2 else 0)
                                    masked_exp_after.append(gei + 1)

                                    def _mask(ve, rel0=rel0, wv=tanh_v, mi=mi,
                                              gti=gti, slot_w=slot_w):
                                        ve.wait_ge(sems["at_tanh"], wv)
                                        if slot_w >= 1:
                                            ve.wait_ge(sems["at_exp"], slot_w)
                                        t1 = None
                                        for s in range(2):
                                            rel = rel0 + s
                                            w = (rel + 1) * 128
                                            t1 = ve.tensor_tensor(
                                                tmg[mi % 2][:, s * 512:s * 512 + w],
                                                ttg[gti % 2][:, s * 512:s * 512 + w],
                                                masks[:, rel * 512:rel * 512 + w],
                                                ALU.add)
                                            if w < 512:
                                                t1 = ve.tensor_copy(
                                                    tmg[mi % 2][:, s * 512 + w:(s + 1) * 512],
                                                    ttg[gti % 2][:, s * 512 + w:(s + 1) * 512])
                                        t1.then_inc(sems["at_mask"], 1)
                                    ph.add("vector", _mask)
                                    mask_v = inc("at_mask", 1)
                                else:
                                    mask_v = None

                                def _exp(sc, masked=masked, mask_v=mask_v,
                                         gti=gti, gei=gei,
                                         mi=(cnt["at_mask"] - 1 if masked else 0)):
                                    if masked:
                                        sc.wait_ge(sems["at_mask"], mask_v)
                                    if gei >= 2:
                                        sc.wait_ge(sems["at_av"], gei - 1)
                                    src = tmg[mi % 2][:] if masked else ttg[gti % 2][:]
                                    sc.activation(pTg[gei % 2][:], src, AF.Exp,
                                                  scale=50.0).then_inc(sems["at_exp"], 1)
                                ph.add("scalar", _exp)
                                inc("at_exp", 1)

                            av_full = cnt["at_av"]

                            def _scl(ve, sl=sl, qb=qb, wv=av_full):
                                ve.wait_ge(sems["at_av"], wv)
                                ve.reciprocal(rrow[:], psR)
                                ve.tensor_tensor(srow[:], rrow[:],
                                                 grow[sl][:, qb * 512:(qb + 1) * 512],
                                                 ALU.mult).then_inc(sems["at_scl"], 1)
                            ph.add("vector", _scl)
                            inc("at_scl", 1)

                            bc_v = cnt["at_bc"]

                            def _bcev(ve, wv=bc_v, gqb=gqb, otw=gqb - 1):
                                ve.wait_ge(sems["at_bc"], wv)
                                if gqb >= 2:
                                    ve.wait_ge(sems["at_ot"], otw)
                                ve.tensor_copy(bcsb[gqb % 2][:], psBc
                                               ).then_inc(sems["at_bcev"], 1)
                            ph.add("vector", _bcev)
                            bcev_v = inc("at_bcev", 1)

                            def _otev(ve, h=h, qb=qb, wv=bcev_v, gqb=gqb):
                                ve.wait_ge(sems["at_bcev"], wv)
                                ve.tensor_tensor(
                                    oaT[:, h * N + qb * 512:h * N + (qb + 1) * 512],
                                    psO, bcsb[gqb % 2][:], ALU.mult
                                    ).then_inc(sems["at_ot"], 1)
                            ph.add("vector", _otev)
                            inc("at_ot", 1)

                    barrier(ph)
                    ph.flush(block)

                    ph = Phase()
                    b_smm = cnt["s4_mm"]
                    b_sev = cnt["s4_ev"]
                    b_sst = cnt["s4_st"]
                    for dj in range(4):
                        def _wodld(gp, b=b, dj=dj, mmw=b_smm + 16 * (dj - 1)):
                            if dj >= 2:
                                gp.wait_ge(sems["s4_mm"], mmw)
                            gp.dma_start(
                                out=woutd[dj % 2][:].rearrange("p (hc m) -> p hc m", hc=H),
                                in_=wout_in[:, dj * 512:(dj + 1) * 512
                                            ].rearrange("(hc p) m -> p hc m", p=128)
                            ).then_inc(sems["w_ld"], 16)
                        ph.add("gpsimd", _wodld)
                        wod_v = inc("w_ld", 16)

                        for tt in range(16):
                            i4 = dj * 16 + tt

                            def _smm(te, dj=dj, tt=tt, i4=i4, wv=wod_v,
                                     evw=b_sev + i4 - 3):
                                te.wait_ge(sems["w_ld"], wv)
                                if i4 >= 4:
                                    te.wait_ge(sems["s4_ev"], evw)
                                pb = psA[:, (i4 % 4) * 512:(i4 % 4 + 1) * 512]
                                last = None
                                for hc in range(H):
                                    last = te.matmul(
                                        pb,
                                        oaT[:, hc * N + tt * 128:hc * N + (tt + 1) * 128],
                                        woutd[dj % 2][:].rearrange(
                                            "p (hc m) -> p hc m", hc=H)[:, hc, :],
                                        start=(hc == 0), stop=(hc == H - 1))
                                last.then_inc(sems["s4_mm"], 1)
                            ph.add("tensor", _smm)
                            sm_v = inc("s4_mm", 1)

                            def _sev(sc, i4=i4, wv=sm_v, stw=b_sst + 16 * (i4 - 1)):
                                sc.wait_ge(sems["s4_mm"], wv)
                                if i4 >= 2:
                                    sc.wait_ge(sems["s4_st"], stw)
                                pb = psA[:, (i4 % 4) * 512:(i4 % 4 + 1) * 512]
                                sc.copy(stageB[i4 % 2][:], pb).then_inc(sems["s4_ev"], 1)
                            ph.add("scalar", _sev)
                            se_v = inc("s4_ev", 1)

                            def _sst(sy, b=b, dj=dj, tt=tt, i4=i4, wv=se_v):
                                sy.wait_ge(sems["s4_ev"], wv)
                                sy.dma_start(out=s_s[b, tt * 128:(tt + 1) * 128,
                                                     dj * 512:(dj + 1) * 512],
                                             in_=stageB[i4 % 2][:]).then_inc(sems["s4_st"], 16)
                            ph.add("sync", _sst)
                            inc("s4_st", 16)
                    barrier(ph)
                    ph.flush(block)
        with ExitStack() as sc2:
            e2 = sc2.enter_context
            wtile = [e2(nc.sbuf_tensor(f"wtile{i}", [128, D], F32)) for i in range(2)]
            absp = e2(nc.sbuf_tensor("absp", [128, 128], F32))
            redv = e2(nc.sbuf_tensor("redv", [128, 2], F32))
            onesf = e2(nc.sbuf_tensor("onesf", [128, 1], F32))
            tau11 = e2(nc.sbuf_tensor("tau11", [1, 4], F32))
            tauA = e2(nc.sbuf_tensor("tauA", [1, 2], F32))
            tau128 = e2(nc.sbuf_tensor("tau128", [128, 4], F32))
            terna = e2(nc.sbuf_tensor("terna", [128, D], F32))
            ternb = e2(nc.sbuf_tensor("ternb", [128, D], F32))
            ternt = [e2(nc.sbuf_tensor(f"ternt{i}", [128, D], BF16)) for i in range(2)]
            tstage = [e2(nc.sbuf_tensor(f"tstage{i}", [128, D], BF16)) for i in range(2)]
            psT = [e2(nc.psum_tensor(f"psT{i}", [128, 512], BF16)) for i in range(2)]
            psSc = e2(nc.psum_tensor("psSc", [128, 4], F32))

            ph = Phase()

            def _abssrc(i):
                if i < 64:
                    return w1_in[i * 128:(i + 1) * 128, :]
                j = i - 64
                dc, c = j // 4, j % 4
                return w2_in[dc * 128:(dc + 1) * 128, c * D:(c + 1) * D]

            def _absload(sy, par=0):
                for i in range(par, 128, 2):
                    if i >= 2:
                        sy.wait_ge(sems["t5_abs"], i - 1)
                    sy.dma_start(out=wtile[i % 2][:], in_=_abssrc(i)
                                 ).then_inc(sems["t5_ld" if par == 0 else "t5_ld2"], 16)
            ph.add("sync", _absload)
            ph.add("gpsimd", lambda gp: _absload(gp, 1))
            for i in range(64):
                inc("t5_ld", 16)
                inc("t5_ld2", 16)

            def _absred(ve):
                ve.memset(onesf[:], 1.0)
                for i in range(128):
                    ve.wait_ge(sems["t5_ld" if i % 2 == 0 else "t5_ld2"],
                               (i // 2 + 1) * 16)
                    ve.tensor_reduce(absp[:, i:i + 1], wtile[i % 2][:], AX.X,
                                     ALU.add, apply_absolute_value=True
                                     ).then_inc(sems["t5_abs"], 1)
                    ve.wait_ge(sems["t5_abs"], i + 1)
                ve.tensor_reduce(redv[:, 0:1], absp[:, 0:64], AX.X, ALU.add
                                 ).then_inc(sems["t5_abs"], 1)
                ve.wait_ge(sems["t5_abs"], 129)
                ve.tensor_reduce(redv[:, 1:2], absp[:, 64:128], AX.X, ALU.add
                                 ).then_inc(sems["t5_abs"], 1)
            ph.add("vector", _absred)
            for i in range(130):
                inc("t5_abs", 1)

            def _taumm(te, wv=cnt["t5_abs"]):
                te.wait_ge(sems["t5_abs"], wv)
                te.matmul(psSc[0:1, 0:1], redv[:, 0:1], onesf[:], start=True, stop=True)
                te.matmul(psSc[0:1, 1:2], redv[:, 1:2], onesf[:], start=True,
                          stop=True).then_inc(sems["t5_tau"], 1)
            ph.add("tensor", _taumm)
            t1v = inc("t5_tau", 1)

            def _tauv(sc, wv=t1v):
                sc.wait_ge(sems["t5_tau"], wv)
                sc.activation(tauA[0:1, :], psSc[0:1, 0:2], AF.Relu,
                              scale=1.0 / (DFF * D), bias=taub[0:1, 0:1])
                sc.activation(tau11[0:1, 0:2], tauA[0:1, :], AF.Identity,
                              scale=0.5, bias=taub[0:1, 1:2])
                sc.activation(tau11[0:1, 2:4], tauA[0:1, :], AF.Identity,
                              scale=-0.5, bias=taub[0:1, 2:3]).then_inc(sems["t5_tau"], 1)
            ph.add("scalar", _tauv)
            t2v = inc("t5_tau", 1)

            def _taubc(te, wv=t2v):
                te.wait_ge(sems["t5_tau"], wv)
                te.matmul(psSc[:, 0:4], ones_row_f[:], tau11[0:1, 0:4],
                          start=True, stop=True).then_inc(sems["t5_tau"], 1)
            ph.add("tensor", _taubc)
            t3v = inc("t5_tau", 1)

            def _tauev(sc, wv=t3v):
                sc.wait_ge(sems["t5_tau"], wv)
                sc.copy(tau128[:], psSc[:, 0:4]).then_inc(sems["t5_tau"], 1)
            ph.add("scalar", _tauev)
            inc("t5_tau", 1)
            tau_ready = cnt["t5_tau"]
            abs_done = cnt["t5_abs"]

            t5_tern_b = cnt["t5_tern"]
            t5_tp_b = cnt["t5_tp"]
            t5_ev_b = cnt["t5_ev"]
            t5_st_b = cnt["t5_st"]
            for mi in range(2):
                for fc in range(FCN):
                    i = mi * FCN + fc

                    qsem = "t5_ld" if i % 2 == 0 else "t5_ld2"

                    def _tld(sy, mi=mi, fc=fc, i=i, qsem=qsem,
                             tw=t5_tern_b + i - 1, aw=abs_done):
                        if i < 2:
                            sy.wait_ge(sems["t5_abs"], aw)
                        else:
                            sy.wait_ge(sems["t5_tern"], tw)
                        if mi == 0:
                            sy.dma_start(out=wtile[i % 2][:],
                                         in_=w1_in[fc * 128:(fc + 1) * 128, :]
                                         ).then_inc(sems[qsem], 16)
                        else:
                            sy.dma_start(
                                out=wtile[i % 2][:].rearrange("p (dc f) -> p dc f", dc=DC),
                                in_=w2_in[:, fc * 128:(fc + 1) * 128
                                          ].rearrange("(dc p) f -> p dc f", p=128)
                            ).then_inc(sems[qsem], 16)
                    ph.add("sync" if i % 2 == 0 else "gpsimd", _tld)
                    tld_v = inc(qsem, 16)

                    def _tern(ve, mi=mi, i=i, wv=tld_v, taw=tau_ready,
                              qsem=qsem, tpw=t5_tp_b + 4 * (i - 1)):
                        ve.wait_ge(sems[qsem], wv)
                        if i == 0:
                            ve.wait_ge(sems["t5_tau"], taw)
                        if i >= 2:
                            ve.wait_ge(sems["t5_tp"], tpw)
                        ve.tensor_scalar(terna[:], wtile[i % 2][:],
                                         tau128[:, mi:mi + 1], 1.0,
                                         ALU.is_gt, ALU.subtract)
                        ve.tensor_scalar(ternb[:], wtile[i % 2][:],
                                         tau128[:, 2 + mi:3 + mi], None, ALU.is_ge)
                        ve.tensor_tensor(ternt[i % 2][:], terna[:], ternb[:],
                                         ALU.add).then_inc(sems["t5_tern"], 1)
                    ph.add("vector", _tern)
                    tern_v = inc("t5_tern", 1)

                    for g in range(4):
                        gi = i * 4 + g

                        def _ttp(te, i=i, g=g, gi=gi, wv=tern_v,
                                 evw=t5_ev_b + gi - 1):
                            te.wait_ge(sems["t5_tern"], wv)
                            if gi >= 2:
                                te.wait_ge(sems["t5_ev"], evw)
                            last = None
                            for j in range(4):
                                dc = g * 4 + j
                                last = te.transpose(psT[gi % 2][:, j * 128:(j + 1) * 128],
                                                    ternt[i % 2][:, dc * 128:(dc + 1) * 128],
                                                    identb[:])
                            last.then_inc(sems["t5_tp"], 1)
                        ph.add("tensor", _ttp)
                        tp_v = inc("t5_tp", 1)

                        def _tev(sc, i=i, g=g, gi=gi, wv=tp_v,
                                 stw=t5_st_b + 16 * (i - 1)):
                            sc.wait_ge(sems["t5_tp"], wv)
                            if g == 0 and i >= 2:
                                sc.wait_ge(sems["t5_st"], stw)
                            sc.copy(tstage[i % 2][:].rearrange(
                                "p (g4 j t) -> p g4 j t", g4=4, j=4)[:, g],
                                psT[gi % 2][:].rearrange("p (j t) -> p j t", j=4)
                                ).then_inc(sems["t5_ev"], 1)
                        ph.add("scalar", _tev)
                        inc("t5_ev", 1)

                    def _tst(sy, mi=mi, fc=fc, i=i, wv=cnt["t5_ev"]):
                        sy.wait_ge(sems["t5_ev"], wv)
                        if mi == 0:
                            dst = w1qT_s[:, :, fc * 128:(fc + 1) * 128
                                         ].rearrange("dc p f -> p dc f")
                            sy.dma_start(out=dst, in_=tstage[i % 2][:].rearrange(
                                "p (dc f) -> p dc f", dc=DC)).then_inc(sems["t5_st"], 16)
                        else:
                            sy.dma_start(out=w2qT_s[fc * 128:(fc + 1) * 128, :],
                                         in_=tstage[i % 2][:]).then_inc(sems["t5_st"], 16)
                    ph.add("scalar", _tst)
                    inc("t5_st", 16)

            barrier(ph)
            ph.flush(block)

        with ExitStack() as sc3:
            e3 = sc3.enter_context
            xnb = e3(nc.sbuf_tensor("xnb", [128, DC * 1024], BF16))
            w1sl = [e3(nc.sbuf_tensor(f"w1sl{i}", [128, DC * 256], BF16))
                    for i in range(2)]
            w2gb = e3(nc.sbuf_tensor("w2gb", [128, 8 * D], BF16))
            hTg = e3(nc.sbuf_tensor("hTg", [128, 8 * 1024], BF16))
            y_acc = e3(nc.sbuf_tensor("y_acc", [128, 8 * D], BF16))
            b2b = e3(nc.sbuf_tensor("b2b", [128, D], BF16))
            yb = e3(nc.sbuf_tensor("yb", [128, D], F32))
            gg = e3(nc.sbuf_tensor("gg", [128, D], BF16))
            sld = [e3(nc.sbuf_tensor(f"sld{i}", [128, D], BF16)) for i in range(2)]
            xld = [e3(nc.sbuf_tensor(f"xld{i}", [128, D], F32)) for i in range(2)]
            ost = [e3(nc.sbuf_tensor(f"ost{i}", [128, D], F32)) for i in range(2)]
            psH = [e3(nc.psum_tensor(f"psH{i}", [128, 1024], F32)) for i in range(2)]
            psY = [e3(nc.psum_tensor(f"psY{i}", [128, 512], F32)) for i in range(2)]

            xnb_view = xnb[:].rearrange("p (dc t) -> p dc t", dc=DC)

            ph = Phase()

            def _b2mm(te):
                for dj in range(4):
                    if dj >= 2:
                        te.wait_ge(sems["f6_yev"], dj - 1)
                    te.matmul(psY[dj % 2][:], ones_row_f[:],
                              b2row[0:1, dj * 512:(dj + 1) * 512],
                              start=True, stop=True).then_inc(sems["f6_y"], 1)
            ph.add("tensor", _b2mm)

            def _b2ev(sc):
                for dj in range(4):
                    sc.wait_ge(sems["f6_y"], dj + 1)
                    sc.copy(b2b[:, dj * 512:(dj + 1) * 512], psY[dj % 2][:]
                            ).then_inc(sems["f6_yev"], 1)
            ph.add("scalar", _b2ev)
            for dj in range(4):
                inc("f6_y", 1)
                inc("f6_yev", 1)
            barrier(ph)
            ph.flush(block)

            NG = 8
            GC = 8
            gsl_ctr = [0]
            for blk in range(4):
                b = blk // 2
                t0 = (blk % 2) * 1024
                ph = Phase()
                h_prev_blk = cnt["f6_h"]

                def _xnbld(sy, b=b, t0=t0, hw=h_prev_blk):
                    if hw >= 1:
                        sy.wait_ge(sems["f6_h"], hw)
                    sy.dma_start(out=xnb_view,
                                 in_=xnT_s[b][:, :, t0:t0 + 1024
                                              ].rearrange("dc p t -> p dc t")
                                 ).then_inc(sems["f6_xnb"], 16)
                ph.add("sync", _xnbld)
                xnb_v = inc("f6_xnb", 16)

                import os as _os
                _nglim = int(_os.environ.get("K_NGLIM", "99"))
                for g in range(NG):
                    if blk == 0 and g >= _nglim:
                        continue
                    y_prev = cnt["f6_y"]

                    for ss in range(4):
                        gsl = gsl_ctr[0]
                        gsl_ctr[0] += 1
                        sg = g * 4 + ss

                        def _w1ld(sy, sg=sg, gsl=gsl, hw=2 * (gsl - 1)):
                            if gsl >= 2:
                                sy.wait_ge(sems["f6_h"], hw)
                            sy.dma_start(
                                out=w1sl[gsl % 2][:].rearrange("p (dc f) -> p dc f", dc=DC),
                                in_=w1qT_s[:, :, sg * 256:(sg + 1) * 256
                                           ].rearrange("dc p f -> p dc f")
                            ).then_inc(sems["f6_w1"], 16)
                        ph.add("sync", _w1ld)
                        inc("f6_w1", 16)
                    w1_vals = [cnt["f6_w1"] - 16 * (3 - s) for s in range(4)]

                    def _w2ld(sc, g=g, yw=y_prev):
                        if yw >= 1:
                            sc.wait_ge(sems["f6_y"], yw)
                        for j in range(GC):
                            fc = g * GC + j
                            sc.dma_start(out=w2gb[:, j * D:(j + 1) * D],
                                         in_=w2qT_s[fc * 128:(fc + 1) * 128, :]
                                         ).then_inc(sems["f6_w2"], 16)
                    ph.add("scalar", _w2ld)
                    inc("f6_w2", 16 * GC)
                    w2_v = cnt["f6_w2"]

                    gelu0 = cnt["f6_gelu"]
                    for j in range(GC):
                        fc = g * GC + j
                        gh = cnt["f6_h"]

                        def _hmm(te, j=j, gh=gh, xv=xnb_v, yw=y_prev,
                                 wv=w1_vals[j // 2], gsl_base=gsl_ctr[0] - 4,
                                 geluw=gh - 1):
                            te.wait_ge(sems["f6_w1"], wv)
                            if j == 0:
                                te.wait_ge(sems["f6_xnb"], xv)
                                if yw >= 1:
                                    te.wait_ge(sems["f6_y"], yw)
                            if gh >= 2:
                                te.wait_ge(sems["f6_gelu"], geluw)
                            last = None
                            for half in range(2):
                                for dc in range(DC):
                                    last = te.matmul(
                                        psH[gh % 2][:, half * 512:(half + 1) * 512],
                                        w1sl[(gsl_base + j // 2) % 2][:].rearrange(
                                            "p (dc f) -> p dc f", dc=DC)[
                                            :, dc, (j % 2) * 128:(j % 2 + 1) * 128],
                                        xnb_view[:, dc, half * 512:(half + 1) * 512],
                                        start=(dc == 0), stop=(dc == DC - 1))
                            last.then_inc(sems["f6_h"], 1)
                        ph.add("tensor", _hmm)
                        h_v = inc("f6_h", 1)

                        def _gelu(sc, j=j, fc=fc, gh=gh, wv=h_v):
                            sc.wait_ge(sems["f6_h"], wv)
                            sc.activation(hTg[:, j * 1024:(j + 1) * 1024],
                                          psH[gh % 2][:], AF.Gelu,
                                          bias=b1_sb[:, fc:fc + 1]
                                          ).then_inc(sems["f6_gelu"], 1)
                        ph.add("scalar", _gelu)
                        inc("f6_gelu", 1)

                    gelu_all = cnt["f6_gelu"]

                    for tt in range(8):
                        for dj in range(4):
                            gy = cnt["f6_y"]

                            def _ymm(te, tt=tt, dj=dj, gy=gy, gw=gelu_all,
                                     w2v=w2_v, yevw=gy - 1,
                                     first=(tt == 0 and dj == 0)):
                                if first:
                                    te.wait_ge(sems["f6_gelu"], gw)
                                    te.wait_ge(sems["f6_w2"], w2v)
                                if gy >= 2:
                                    te.wait_ge(sems["f6_yev"], yevw)
                                last = None
                                for j in range(GC):
                                    last = te.matmul(
                                        psY[gy % 2][:],
                                        hTg[:, j * 1024 + tt * 128:
                                            j * 1024 + (tt + 1) * 128],
                                        w2gb[:, j * D + dj * 512:j * D + (dj + 1) * 512],
                                        start=(j == 0), stop=(j == GC - 1))
                                last.then_inc(sems["f6_y"], 1)
                            ph.add("tensor", _ymm)
                            y_v = inc("f6_y", 1)

                            def _yev(ve, g=g, tt=tt, dj=dj, gy=gy, wv=y_v,
                                     siluw=cnt["f6_silu"]):
                                ve.wait_ge(sems["f6_y"], wv)
                                ya = y_acc[:, tt * D + dj * 512:tt * D + (dj + 1) * 512]
                                if g == 0:
                                    ve.tensor_tensor(ya, psY[gy % 2][:],
                                                     b2b[:, dj * 512:(dj + 1) * 512],
                                                     ALU.add).then_inc(sems["f6_yev"], 1)
                                elif g < NG - 1:
                                    ve.tensor_tensor(ya, ya, psY[gy % 2][:], ALU.add
                                                     ).then_inc(sems["f6_yev"], 1)
                                else:
                                    if siluw >= 1 and dj == 0:
                                        ve.wait_ge(sems["f6_silu"], siluw)
                                    ve.tensor_tensor(yb[:, dj * 512:(dj + 1) * 512],
                                                     ya, psY[gy % 2][:], ALU.add
                                                     ).then_inc(sems["f6_yev"], 1)
                            ph.add("vector", _yev)
                            inc("f6_yev", 1)

                        if g == NG - 1 and _os.environ.get("K_NOCMB", "0") != "1":
                            gtt = blk * 8 + tt

                            def _sxld(sy, b=b, t0=t0, tt=tt, gtt=gtt,
                                      ow=16 * (gtt - 1)):
                                if gtt >= 2:
                                    sy.wait_ge(sems["f6_out"], ow)
                                r0 = t0 + tt * 128
                                if _op_real("xld"):
                                    sy.dma_start(out=xld[gtt % 2][:],
                                                 in_=x_in[b, r0:r0 + 128, :]
                                                 ).then_inc(sems["f6_xld"], 16)
                                else:
                                    sy.sem_inc(sems["f6_xld"], 16)
                            ph.add("sync", _sxld)
                            xld_v = inc("f6_xld", 16)

                            def _sldf(sc, b=b, t0=t0, tt=tt, gtt=gtt,
                                      ow=16 * (gtt - 1)):
                                if gtt >= 2:
                                    sc.wait_ge(sems["f6_out"], ow)
                                r0 = t0 + tt * 128
                                if _op_real("sld"):
                                    sc.dma_start(out=sld[gtt % 2][:],
                                                 in_=s_s[b, r0:r0 + 128, :]
                                                 ).then_inc(sems["f6_sld"], 16)
                                else:
                                    sc.sem_inc(sems["f6_sld"], 16)
                            if _os.environ.get("K_CMBV", "0") == "1":
                                ph.add("sync", _sldf)
                            else:
                                ph.add("scalar", _sldf)
                            sld_v = inc("f6_sld", 16)

                            yev_all = cnt["f6_yev"]

                            def _silu(sc, wv=yev_all, cw=cnt["f6_cmb"]):
                                sc.wait_ge(sems["f6_yev"], wv)
                                if cw >= 1:
                                    sc.wait_ge(sems["f6_cmb"], cw)
                                if _op_real("silu"):
                                    sc.activation(gg[:], yb[:], AF.Silu
                                                  ).then_inc(sems["f6_silu"], 1)
                                elif _op_real("silucp"):
                                    sc.copy(gg[:], yb[:]).then_inc(sems["f6_silu"], 1)
                                elif _op_real("silujunk"):
                                    sc.copy(ost[0][:], xnb[:, 0:D]).then_inc(sems["f6_silu"], 1)
                                elif _op_real("siluyb"):
                                    sc.copy(ost[0][:], yb[:]).then_inc(sems["f6_silu"], 1)
                                elif _op_real("silugg"):
                                    sc.copy(gg[:], xnb[:, 0:D]).then_inc(sems["f6_silu"], 1)
                                else:
                                    sc.sem_inc(sems["f6_silu"], 1)
                            ph.add("scalar", _silu)
                            silu_v = inc("f6_silu", 1)

                            def _cmb(ve, gtt=gtt, wv=silu_v, sv=sld_v, xv=xld_v,
                                     ow=16 * (gtt - 1)):
                                ve.wait_ge(sems["f6_silu"], wv)
                                ve.wait_ge(sems["f6_sld"], sv)
                                ve.wait_ge(sems["f6_xld"], xv)
                                if gtt >= 2:
                                    ve.wait_ge(sems["f6_out"], ow)
                                if _op_real("cmb"):
                                    ve.tensor_tensor(gg[:], gg[:], sld[gtt % 2][:],
                                                     ALU.mult)
                                    ve.tensor_tensor(ost[gtt % 2][:], gg[:],
                                                     xld[gtt % 2][:], ALU.add
                                                     ).then_inc(sems["f6_cmb"], 1)
                                elif _op_real("cmbcp"):
                                    ve.tensor_copy(ost[gtt % 2][:], xld[gtt % 2][:]
                                                   ).then_inc(sems["f6_cmb"], 1)
                                else:
                                    ve.sem_inc(sems["f6_cmb"], 1)
                            ph.add("vector", _cmb)
                            cmb_v = inc("f6_cmb", 1)

                            def _ost(sy, b=b, t0=t0, tt=tt, gtt=gtt, wv=cmb_v):
                                sy.wait_ge(sems["f6_cmb"], wv)
                                r0 = t0 + tt * 128
                                if _op_real("ost"):
                                    sy.dma_start(out=out[b, r0:r0 + 128, :],
                                                 in_=ost[gtt % 2][:]
                                                 ).then_inc(sems["f6_out"], 16)
                                else:
                                    sy.sem_inc(sems["f6_out"], 16)
                            ph.add("sync", _ost)
                            inc("f6_out", 16)

                barrier(ph)
                ph.flush(block)

            ph = Phase()
            out_total = cnt["f6_out"]

            def _fin(sy, wv=out_total):
                sy.wait_ge(sems["f6_out"], wv)
            ph.add("sync", _fin)
            ph.flush(block)

    return nc


_CACHED = {}


def kernel(L0_attn_x, L0_attn_w_qkv, L0_attn_w_gates, L0_attn_w_out,
           L0_ffn_w1, L0_ffn_b1, L0_ffn_w2, L0_ffn_b2):
    x = np.ascontiguousarray(np.asarray(L0_attn_x, dtype=np.float32))
    wqkv_a = np.ascontiguousarray(np.asarray(L0_attn_w_qkv, dtype=np.float32))
    wg_a = np.ascontiguousarray(np.asarray(L0_attn_w_gates, dtype=np.float32))
    wout_a = np.ascontiguousarray(np.asarray(L0_attn_w_out, dtype=np.float32))
    w1_a = np.ascontiguousarray(np.asarray(L0_ffn_w1, dtype=np.float32))
    b1_a = np.ascontiguousarray(np.asarray(L0_ffn_b1, dtype=np.float32))
    w2_a = np.ascontiguousarray(np.asarray(L0_ffn_w2, dtype=np.float32))
    b2_a = np.ascontiguousarray(np.asarray(L0_ffn_b2, dtype=np.float32))

    if "nc" not in _CACHED:
        _CACHED["nc"] = build()
    nc = _CACHED["nc"]

    in_maps = [{"x": x[c * B:(c + 1) * B], "wqkv": wqkv_a, "wg": wg_a,
                "wout": wout_a, "w1": w1_a, "b1": b1_a, "w2": w2_a, "b2": b2_a}
               for c in range(NCORES)]
    res = run_bass_kernel_spmd(nc, in_maps, core_ids=list(range(NCORES)))
    return np.concatenate([r["out"] for r in res.results], axis=0)
